# version 13
# baseline (speedup 1.0000x reference)
import numpy as np
import contextlib
import concourse.bass as bass
import concourse.mybir as mybir
from concourse.bass_utils import run_bass_kernel_spmd

F32 = mybir.dt.float32
BF16 = mybir.dt.bfloat16
AF = mybir.ActivationFunctionType
ALU = mybir.AluOpType
AX = mybir.AxisListType

NEG = -1e30
D = 4096
HD = 128
NH = 32
NKV = 8
S = 2048
NUNIT = 8
OWN = 1024


def _units(p):
    return [2 * i + p for i in range(NUNIT)]


def _nk(i):
    return 256 * (i + 1)


def _w(i):
    return min(384, _nk(i))


class P:
    def __init__(self):
        self.nc = bass.Bass("TRN2", target_bir_lowering=False)
        self.es = contextlib.ExitStack()
        self.cnt = {}
        self.sems = {}

    def sem(self, name):
        h = self.es.enter_context(self.nc.semaphore(name))
        self.cnt[name] = 0
        self.sems[name] = h
        return h

    def inc(self, instr, name, k=1):
        instr.then_inc(self.sems[name], k)
        self.cnt[name] += k
        return self.cnt[name]

    def w(self, eng, name, val=None):
        eng.wait_ge(self.sems[name], self.cnt[name] if val is None else val)


def build(p: P):
    nc = p.nc
    din = lambda n, sh, dt=F32: nc.dram_tensor(n, sh, dt, kind="ExternalInput").ap()
    xo = din("xo", [OWN, D])
    xoT = din("xoT", [D, OWN])
    xbT = din("xbT", [D, S])
    wqT = din("wqT", [D, D])
    wkT = din("wkT", [D, 1024])
    wvT = din("wvT", [D, 1024])
    woT = din("woT", [D, D])
    aT = din("aT", [D, 1024])
    wrT = din("wrT", [D, 64])
    bF = din("bF", [1024, D])
    cosq = din("cosq", [HD, OWN])
    sinq = din("sinq", [HD, OWN])
    cosk = din("cosk", [HD, S])
    sink = din("sink", [HD, S])
    cnegs = din("cnegs", [NUNIT, 128, 384], BF16)
    fixs = din("fixs", [NUNIT, 128, 384], BF16)
    ident = din("ident", [128, 128], BF16)
    rotPT = din("rotPT", [128, 128], BF16)
    exp16 = din("exp16", [64, 1024], BF16)
    e16g = din("e16g", [128, 2048], BF16)
    y = nc.dram_tensor("y", [OWN, D], F32, kind="ExternalOutput").ap()

    oscr = nc.dram_tensor("oscr", [D, OWN], BF16, kind="Internal").ap()
    wooscr = nc.dram_tensor("wooscr", [D, OWN], BF16, kind="Internal").ap()
    x1scr = nc.dram_tensor("x1scr", [OWN, D], F32, kind="Internal").ap()
    x1Tscr = nc.dram_tensor("x1Tscr", [D, OWN], BF16, kind="Internal").ap()
    ffnoscr = nc.dram_tensor("ffnoscr", [D, OWN], BF16, kind="Internal").ap()

    SEMS = ("QA0", "QA1", "QA2", "QA3", "QB0", "QB1", "QB2", "QB3", "QC", "QD", "SV", "SS", "ST", "SG")
    for s_ in SEMS:
        p.sem(s_)
    sync, gp, vec, sc, te = nc.sync, nc.gpsimd, nc.vector, nc.scalar, nc.tensor

    def barrier():
        p.w(vec, "SV")
        p.inc(vec.nop(), "SV")
        p.w(sc, "SS")
        p.inc(sc.nop(), "SS")
        p.w(te, "ST")
        p.inc(te.nop(), "ST")
        p.w(gp, "SG")
        p.inc(gp.nop(), "SG")
        for eng in (sync, gp, vec, sc, te):
            for s_ in SEMS:
                p.w(eng, s_)

    sb = lambda n, sh, dt: p.es.enter_context(nc.sbuf_tensor(n, sh, dt))
    Kfm = sb("Kfm", [128, NKV * S], BF16)
    Vtm = sb("Vtm", [128, 16 * 1024], BF16)
    Qfm = sb("Qfm", [128, NH * OWN], BF16)
    cnegB = sb("cnegB", [128, NUNIT * 384], BF16)
    fixB = sb("fixB", [128, NUNIT * 384], BF16)
    identb = sb("identb", [128, 128], BF16)
    rotb = sb("rotb", [128, 128], BF16)
    exp16b = sb("exp16b", [64, 1024], BF16)
    e16gb = sb("e16gb", [128, 2048], BF16)

    for i in range(NUNIT):
        p.inc(sync.dma_start(out=cnegB[:, i * 384:(i + 1) * 384], in_=cnegs[i]), "QC", 16)
        p.inc(sync.dma_start(out=fixB[:, i * 384:(i + 1) * 384], in_=fixs[i]), "QC", 16)
    p.inc(sync.dma_start(out=identb[:], in_=ident), "QC", 16)
    p.inc(sync.dma_start(out=rotb[:], in_=rotPT), "QC", 16)
    p.inc(sync.dma_start(out=exp16b[:], in_=exp16), "QC", 16)
    p.inc(sync.dma_start(out=e16gb[:], in_=e16g), "QC", 16)
    p.w(te, "QC"); p.w(vec, "QC"); p.w(sc, "QC")

    with contextlib.ExitStack() as s1:
        sb1 = lambda n, sh, dt: s1.enter_context(nc.sbuf_tensor(n, sh, dt))
        ck = sb1("ck", [HD, S], F32)
        sk = sb1("sk", [HD, S], F32)
        p.w(sync, "QC")
        p.inc(sync.dma_start(out=ck[:], in_=cosk), "QC", 16)
        p.inc(sync.dma_start(out=sk[:], in_=sink), "QC", 16)
        p.w(vec, "QC")
        xbt = [sb1(f"xbt{j}", [128, 512], BF16) for j in range(4)]
        wkvt = [sb1(f"wkvt{j}", [128, 1024], BF16) for j in range(4)]
        vstage = sb1("vstage", [128, 8 * 512], BF16)
        rst1 = sb1("rst1", [128, 512], F32)
        rst2 = sb1("rst2", [128, 512], F32)
        kbst = [sb1(f"kbst{j}", [128, 512], BF16) for j in range(2)]
        guard = {0: 0, 1: 0, 2: 0, 3: 0}
        tr_guard = {0: ("SS", 0), 1: ("SS", 0)}
        for pas in range(2):
            wT = wkT if pas == 0 else wvT
            for tt in range(4):
                with contextlib.ExitStack() as sp:
                    pk = [sp.enter_context(nc.psum_tensor(f"pk{pas}{tt}{j}", [128, 512], F32))
                          for j in range(8)]
                    for ch in range(32):
                        b = ch % 4
                        p.w(gp, "ST", guard[b])
                        p.w(gp, f"QA{b}")
                        p.inc(gp.dma_start(out=xbt[b][:],
                                           in_=xbT[ch * 128:(ch + 1) * 128, tt * 512:(tt + 1) * 512]), f"QA{b}", 16)
                        p.inc(gp.dma_start(out=wkvt[b][:], in_=wT[ch * 128:(ch + 1) * 128, :]), f"QA{b}", 16)
                        p.w(te, f"QA{b}")
                        if ch == 0:
                            p.w(te, "SV")
                            p.w(te, "SS")
                        for kb in range(8):
                            ins = te.matmul(pk[kb][:], wkvt[b][:, kb * 128:(kb + 1) * 128], xbt[b][:],
                                            start=(ch == 0), stop=(ch == 31))
                        guard[b] = p.inc(ins, "ST")
                    if pas == 0:
                        cs = ck[:, tt * 512:(tt + 1) * 512]
                        sn = sk[:, tt * 512:(tt + 1) * 512]
                        rot_done = {0: 0, 1: 0}
                        for kb in range(8):
                            b2 = kb % 2
                            p.w(vec, "ST", rot_done[b2])
                            p.w(vec, "ST")
                            p.w(vec, "SV")
                            cpv = p.inc(vec.tensor_copy(kbst[b2][:], pk[kb][:]), "SV")
                            p.w(te, "SV", cpv)
                            rot_done[b2] = p.inc(te.matmul(pk[kb][:], rotb[:], kbst[b2][:],
                                                           start=True, stop=True), "ST")
                            p.w(vec, "SV")
                            p.inc(vec.tensor_mul(rst1[:], kbst[b2][:], cs), "SV")
                            p.w(vec, "ST", rot_done[b2])
                            p.w(vec, "SV")
                            p.inc(vec.tensor_mul(rst2[:], pk[kb][:], sn), "SV")
                            p.w(vec, "SV")
                            dst = Kfm[:, kb * S + tt * 512: kb * S + (tt + 1) * 512]
                            p.inc(vec.tensor_add(dst, rst1[:], rst2[:]), "SV")
                    else:
                        for kb in range(8):
                            p.w(vec, "ST")
                            p.w(vec, "SV")
                            p.inc(vec.tensor_copy(vstage[:, kb * 512:(kb + 1) * 512], pk[kb][:]), "SV")
                if pas == 1:
                    with contextlib.ExitStack() as sp:
                        trp = [sp.enter_context(nc.psum_tensor(f"trv{tt}{j}", [128, 1024], BF16))
                               for j in range(2)]
                        for tb in range(4):
                            j = tb % 2
                            gsem, gval = tr_guard[j]
                            p.w(te, gsem, gval)
                            p.w(te, "SV")
                            for vb in range(8):
                                ins = te.transpose(trp[j][:, vb * 128:(vb + 1) * 128],
                                                   vstage[:, vb * 512 + tb * 128: vb * 512 + (tb + 1) * 128],
                                                   identb[:])
                            tv = p.inc(ins, "ST")
                            p.w(sc, "ST", tv)
                            dst = Vtm[:, (tt * 4 + tb) * 1024: (tt * 4 + tb + 1) * 1024]
                            tr_guard[j] = ("SS", p.inc(sc.copy(dst, trp[j][:]), "SS"))
                        p.w(te, "SS")
        barrier()

    with contextlib.ExitStack() as s1:
        sb1 = lambda n, sh, dt: s1.enter_context(nc.sbuf_tensor(n, sh, dt))
        cq = sb1("cq", [HD, OWN], F32)
        sq_ = sb1("sq_", [HD, OWN], F32)
        p.w(sync, "QC")
        p.inc(sync.dma_start(out=cq[:], in_=cosq), "QC", 16)
        p.inc(sync.dma_start(out=sq_[:], in_=sinq), "QC", 16)
        p.w(vec, "QC")
        xot = [sb1(f"xot{j}", [128, OWN], BF16) for j in range(4)]
        wqt = [sb1(f"wqt{j}", [128, 512], BF16) for j in range(4)]
        qbst = [sb1(f"qbst{j}", [128, OWN], BF16) for j in range(2)]
        rst1 = sb1("rst1b", [128, 512], F32)
        rst2 = sb1("rst2b", [128, 512], F32)
        guard = {0: 0, 1: 0, 2: 0, 3: 0}
        with contextlib.ExitStack() as sp:
            pq = [sp.enter_context(nc.psum_tensor(f"pq{j}", [128, 512], F32)) for j in range(8)]
            for g in range(8):
                for ch in range(32):
                    b = ch % 4
                    p.w(gp, "ST", guard[b])
                    p.w(gp, f"QA{b}")
                    p.inc(gp.dma_start(out=xot[b][:], in_=xoT[ch * 128:(ch + 1) * 128, :]), f"QA{b}", 16)
                    p.inc(gp.dma_start(out=wqt[b][:],
                                       in_=wqT[ch * 128:(ch + 1) * 128, g * 512:(g + 1) * 512]), f"QA{b}", 16)
                    p.w(te, f"QA{b}")
                    if ch == 0:
                        p.w(te, "SV")
                    for ob in range(4):
                        for hf in range(2):
                            ins = te.matmul(pq[ob * 2 + hf][:], wqt[b][:, ob * 128:(ob + 1) * 128],
                                            xot[b][:, hf * 512:(hf + 1) * 512],
                                            start=(ch == 0), stop=(ch == 31))
                    guard[b] = p.inc(ins, "ST")
                rot_done = {0: 0, 1: 0}
                for ob in range(4):
                    h = g * 4 + ob
                    b2 = ob % 2
                    p.w(vec, "ST", rot_done[b2])
                    p.w(vec, "ST")
                    p.w(vec, "SV")
                    p.inc(vec.tensor_copy(qbst[b2][:, 0:512], pq[ob * 2][:]), "SV")
                    p.w(vec, "SV")
                    cpv = p.inc(vec.tensor_copy(qbst[b2][:, 512:1024], pq[ob * 2 + 1][:]), "SV")
                    p.w(te, "SV", cpv)
                    te.matmul(pq[ob * 2][:], rotb[:], qbst[b2][:, 0:512], start=True, stop=True)
                    p.w(te, "ST")
                    rot_done[b2] = p.inc(te.matmul(pq[ob * 2 + 1][:], rotb[:], qbst[b2][:, 512:1024],
                                                   start=True, stop=True), "ST")
                    for hf in range(2):
                        cs = cq[:, hf * 512:(hf + 1) * 512]
                        sn = sq_[:, hf * 512:(hf + 1) * 512]
                        p.w(vec, "SV")
                        p.inc(vec.tensor_mul(rst1[:], qbst[b2][:, hf * 512:(hf + 1) * 512], cs), "SV")
                        p.w(vec, "ST", rot_done[b2])
                        p.w(vec, "SV")
                        p.inc(vec.tensor_mul(rst2[:], pq[ob * 2 + hf][:], sn), "SV")
                        p.w(vec, "SV")
                        dst = Qfm[:, h * OWN + hf * 512: h * OWN + (hf + 1) * 512]
                        p.inc(vec.tensor_add(dst, rst1[:], rst2[:]), "SV")
        barrier()

    with contextlib.ExitStack() as s2:
        sb2 = lambda n, sh, dt: s2.enter_context(nc.sbuf_tensor(n, sh, dt))
        Ebuf = sb2("Ebuf", [128, 2048], BF16)
        PT = sb2("PT", [128, 16 * 512], BF16)
        Cm = sb2("Cm", [128, 128], F32)
        t8 = sb2("t8", [128, 8], F32)
        thr = sb2("thr", [128, 1], F32)
        ngm = sb2("ngm", [128, 1], F32)
        M01 = sb2("M01", [128, 128], F32)
        mnegb = sb2("mnegb", [128, 128], BF16)
        mnT = sb2("mnT", [128, 128], BF16)
        tqb = sb2("tqb", [128, 384], BF16)
        zparts = sb2("zparts", [128, 8], F32)
        zz = sb2("zz", [128, 1], F32)
        rz = sb2("rz", [128, 1], F32)
        rzb = [sb2(f"rzb{j}", [128, 1], BF16) for j in range(2)]
        dgrz = [sb2(f"dgrz{j}", [128, 128], BF16) for j in range(2)]
        ou = sb2("ou", [128, NH * 128], BF16)
        p.w(vec, "SV")
        p.inc(vec.memset(mnegb[:], 0.0), "SV")
        with contextlib.ExitStack() as sp:
            psS = sp.enter_context(nc.psum_tensor("psS", [128, 2048], F32))
            ptr = [sp.enter_context(nc.psum_tensor(f"ptr{j}", [128, 512], F32)) for j in range(2)]
            ptrM = sp.enter_context(nc.psum_tensor("ptrM", [128, 128], BF16))
            pso = sp.enter_context(nc.psum_tensor("pso", [128, 512], F32))
            bank_free = [("SV", 0)] * 4
            trg = {0: ("SV", 0), 1: ("SV", 0)}
            dg_guard = {0: 0, 1: 0}
            tmt_prev = 0
            tmask_prev = 0
            mnT_g = ("SV", 0)
            pend = None
            pend_o = None
            ss_oevac = 0
            t_av_prev = 0
            t_pt_last = 0

            def emit_pt_av(st):
                nonlocal t_av_prev, t_pt_last
                (i, h, ntile, ss_exp, sg_dg, kvh, hp) = st
                hh = h % 4
                ngr = (ntile + 3) // 4
                for g in range(ngr):
                    j = g % 2
                    nt = min(4, ntile - g * 4)
                    te.ldweights(identb[:])
                    p.w(te, "SG", sg_dg)
                    gs, gv = trg[j]
                    p.w(te, gs, gv)
                    for t in range(nt):
                        kt = g * 4 + t
                        ins = te.matmul(ptr[j][:, t * 128:(t + 1) * 128],
                                        Ebuf[:, kt * 128:(kt + 1) * 128], dgrz[hp][:],
                                        start=True, stop=True)
                    tv = p.inc(ins, "ST")
                    p.w(sc, "ST", tv)
                    if hh == 0 and g == 0:
                        p.w(sc, "ST", t_av_prev)
                    dst = PT[:].rearrange("q2 (kt s) -> q2 kt s", s=512)
                    ev = p.inc(sc.copy(dst[:, g * 4: g * 4 + nt, hh * 128:(hh + 1) * 128],
                                       ptr[j][:, 0:nt * 128].rearrange("q2 (t s) -> q2 t s", s=128)), "SS")
                    trg[j] = ("SS", ev)
                t_pt_last = p.cnt["ST"]
                dg_guard[hp] = t_pt_last
                if hh == 3:
                    ev_all = p.cnt["SS"]
                    for kt in range(ntile):
                        if kt == 0:
                            te.ldweights(identb[:])
                        p.w(te, "SS", ev_all)
                        if kt == 0:
                            p.w(te, "SS", ss_oevac)
                        vsl = Vtm[:, kt * 1024 + kvh * 128: kt * 1024 + (kvh + 1) * 128]
                        ins = te.matmul(pso[:], vsl, PT[:, kt * 512:(kt + 1) * 512],
                                        start=(kt == 0), stop=(kt == ntile - 1))
                    t_av = p.inc(ins, "ST")
                    t_av_prev = t_av
                    return (h, t_av)
                return None

            def emit_oevac(st):
                nonlocal ss_oevac
                (h, t_av) = st
                p.w(sc, "ST", t_av)
                p.w(sc, "SS")
                ss_oevac = p.inc(sc.copy(ou[:, (h - 3) * 128:(h + 1) * 128], pso[:]), "SS")

            for i in range(NUNIT):
                Nk, W = _nk(i), _w(i)
                nch, nbc = Nk // 16, W // 16
                nmid = Nk - W
                ncmid = nmid // 16
                ntile = Nk // 128
                mid_banks = []
                off = 0
                while off < nmid:
                    n = min(512, nmid - off)
                    mid_banks.append((off // 512, off, n))
                    off += n
                base = 0
                for h in range(NH):
                    kvh = h // 4
                    base = (h % 2) * 1024 if Nk <= 1024 else 0
                    qsl = Qfm[:, h * OWN + i * 128: h * OWN + (i + 1) * 128]
                    off = 0
                    while off < Nk:
                        n = min(512, Nk - off)
                        bank = (base + off) // 512
                        gs, gv = bank_free[bank]
                        te.ldweights(identb[:])
                        p.w(te, gs, gv)
                        ins = te.matmul(psS[:, base + off: base + off + n], qsl,
                                        Kfm[:, kvh * S + off: kvh * S + off + n],
                                        start=True, stop=True)
                        p.inc(ins, "ST")
                        off += n
                    off = nmid
                    while off < Nk:
                        n = min(512 - (base + off) % 512, Nk - off)
                        p.w(te, "ST")
                        ins = te.matmul(psS[:, base + off: base + off + n], identb[:],
                                        cnegB[:, i * 384 + (off - nmid): i * 384 + (off - nmid) + n],
                                        start=False, stop=True, skip_group_check=True)
                        off += n
                    ts = p.inc(ins, "ST")
                    if pend is not None:
                        (pi, ph, pntile, pss_exp, pkvh, php, pzcol) = pend
                        p.w(vec, "SS", pss_exp)
                        p.w(vec, "SV")
                        p.inc(vec.tensor_reduce(zz[:], zparts[:, 0:pzcol], AX.X, ALU.add), "SV")
                        p.w(vec, "SV")
                        p.inc(vec.reciprocal(rz[:], zz[:]), "SV")
                        p.w(vec, "ST", dg_guard[php])
                        p.w(vec, "SV")
                        sv_rzb = p.inc(vec.tensor_copy(rzb[php][:], rz[:]), "SV")
                        p.w(gp, "SV", sv_rzb)
                        p.w(gp, "SG")
                        sg_dg = p.inc(gp.affine_select(out=dgrz[php][:],
                                                       in_=rzb[php][:, 0:1].to_broadcast([128, 128]),
                                                       pattern=[[-1, 128]], compare_op=ALU.is_equal,
                                                       fill=0.0, base=0, channel_multiplier=1), "SG")
                        pend_o = emit_pt_av((pi, ph, pntile, pss_exp, sg_dg, pkvh, php))
                        pend = None
                        if pend_o is not None:
                            emit_oevac(pend_o)
                            pend_o = None
                    if nmid > 0:
                        p.w(vec, "ST", ts)
                        p.w(vec, "SV")
                        src = psS[:, base: base + nmid].rearrange("p (c k) -> p c k", k=16)
                        p.inc(vec.tensor_reduce(Cm[:, 0:ncmid], src, AX.X, ALU.max), "SV")
                    p.w(vec, "ST", ts)
                    p.w(vec, "SV")
                    p.inc(vec.tensor_reduce(Cm[:, ncmid:nch],
                                            psS[:, base + nmid: base + Nk].rearrange("p (c k) -> p c k", k=16),
                                            AX.X, ALU.max), "SV")
                    sv_cmb = p.cnt["SV"]
                    p.w(vec, "SV")
                    p.inc(vec.max(out=t8[:], in_=Cm[:, 0:nch]), "SV")
                    p.w(vec, "SV")
                    p.inc(vec.tensor_scalar(ngm[:], t8[:, 0:1], -1.0, None, ALU.mult), "SV")
                    p.w(vec, "SV")
                    p.inc(vec.tensor_reduce(thr[:], t8[:], AX.X, ALU.min), "SV")
                    p.w(vec, "SV")
                    p.inc(vec.tensor_scalar(M01[:, 0:nch], Cm[:, 0:nch], thr[:, 0:1], None, ALU.is_ge), "SV")
                    p.w(vec, "SV")
                    p.inc(vec.memset(M01[:, 0:8], 1.0), "SV")
                    p.w(vec, "ST", tmt_prev)
                    p.w(vec, "SV")
                    p.inc(vec.tensor_scalar(mnegb[:, 0:nch], M01[:, 0:nch], 1.0, 1e30,
                                            ALU.subtract, ALU.mult), "SV")
                    sv_mneg = p.cnt["SV"]
                    mnbb = mnegb[:, nch - nbc:nch].to_broadcast([128, nbc, 16])
                    fx3 = fixB[:, i * 384: i * 384 + W].rearrange("p (c k) -> p c k", k=16)
                    p.w(vec, "ST", tmask_prev)
                    p.w(vec, "SV")
                    sv_tqb = p.inc(vec.tensor_tensor(tqb[:, 0:W].rearrange("p (c k) -> p c k", k=16),
                                                     fx3, mnbb, ALU.max), "SV")
                    if nmid > 0:
                        te.ldweights(identb[:])
                        p.w(te, "SV", sv_mneg)
                        gs, gv = mnT_g
                        p.w(te, gs, gv)
                        tmt_prev = p.inc(te.transpose(ptrM[:], mnegb[:], identb[:]), "ST")
                        p.w(sc, "ST", tmt_prev)
                        ssm = p.inc(sc.copy(mnT[:], ptrM[:]), "SS")
                        mnT_g = ("SS", ssm)
                        p.w(te, "SS", ssm)
                        for (bank, off, n) in mid_banks:
                            ins = te.matmul(psS[:, base + off: base + off + n], mnT[:],
                                            e16gb[:, off:off + n], start=False, stop=True,
                                            skip_group_check=True)
                        p.inc(ins, "ST")
                    p.w(te, "SV", sv_tqb)
                    p.w(te, "SV", sv_cmb)
                    off = nmid
                    while off < Nk:
                        n = min(512 - (base + off) % 512, Nk - off)
                        p.w(te, "ST")
                        ins = te.matmul(psS[:, base + off: base + off + n], identb[:],
                                        tqb[:, off - nmid: off - nmid + n],
                                        start=False, stop=True, skip_group_check=True)
                        off += n
                    t_mask = p.inc(ins, "ST")
                    tmask_prev = t_mask
                    zcol = 0
                    p.w(sc, "ST", t_mask)
                    p.w(sc, "SV", sv_mneg)
                    for (bank, off, n) in mid_banks:
                        p.w(sc, "SS")
                        ssv = p.inc(sc.activation(Ebuf[:, off:off + n], psS[:, base + off: base + off + n],
                                                  AF.Exp, bias=ngm[:, 0:1],
                                                  accum_out=zparts[:, zcol:zcol + 1]), "SS")
                        zcol += 1
                        bank_free[(base + off) // 512] = ("SS", ssv)
                    off = nmid
                    while off < Nk:
                        n = min(512 - (base + off) % 512, Nk - off)
                        p.w(sc, "SS")
                        ssv = p.inc(sc.activation(Ebuf[:, off:off + n], psS[:, base + off: base + off + n],
                                                  AF.Exp, bias=ngm[:, 0:1],
                                                  accum_out=zparts[:, zcol:zcol + 1]), "SS")
                        zcol += 1
                        bank_free[(base + off) // 512] = ("SS", ssv)
                        off += n
                    ss_exp = p.cnt["SS"]
                    pend = (i, h, ntile, ss_exp, kvh, h % 2, zcol)
                (pi, ph, pntile, pss_exp, pkvh, php, pzcol) = pend
                p.w(vec, "SS", pss_exp)
                p.w(vec, "SV")
                p.inc(vec.tensor_reduce(zz[:], zparts[:, 0:pzcol], AX.X, ALU.add), "SV")
                p.w(vec, "SV")
                p.inc(vec.reciprocal(rz[:], zz[:]), "SV")
                p.w(vec, "ST", dg_guard[php])
                p.w(vec, "SV")
                sv_rzb = p.inc(vec.tensor_copy(rzb[php][:], rz[:]), "SV")
                p.w(gp, "SV", sv_rzb)
                p.w(gp, "SG")
                sg_dg = p.inc(gp.affine_select(out=dgrz[php][:],
                                               in_=rzb[php][:, 0:1].to_broadcast([128, 128]),
                                               pattern=[[-1, 128]], compare_op=ALU.is_equal,
                                               fill=0.0, base=0, channel_multiplier=1), "SG")
                emit_oevac(emit_pt_av((pi, ph, pntile, pss_exp, sg_dg, pkvh, php)))
                pend = None
                p.w(sync, "SS")
                p.w(sync, "QD")
                od = oscr.rearrange("(h d) q -> d h q", d=128)
                osrc = ou[:].rearrange("p (h q) -> p h q", q=128)
                for hg in range(4):
                    p.inc(sync.dma_start(out=od[:, hg * 8:(hg + 1) * 8, i * 128:(i + 1) * 128],
                                         in_=osrc[:, hg * 8:(hg + 1) * 8, :]), "QD", 16)
                p.w(sc, "QD")
        barrier()

    with contextlib.ExitStack() as s3:
        sb3 = lambda n, sh, dt: s3.enter_context(nc.sbuf_tensor(n, sh, dt))
        wot = [sb3(f"wot{j}", [128, 512], BF16) for j in range(4)]
        oft = [sb3(f"oft{j}", [128, OWN], BF16) for j in range(4)]
        wst = sb3("wst", [128, OWN], BF16)
        guard = {0: 0, 1: 0, 2: 0, 3: 0}
        with contextlib.ExitStack() as sp:
            pw = [sp.enter_context(nc.psum_tensor(f"pw{j}", [128, 512], F32)) for j in range(8)]
            for g2 in range(8):
                for ch in range(32):
                    b = ch % 4
                    p.w(gp, "ST", guard[b])
                    p.w(sync, "ST", guard[b])
                    p.w(gp, f"QA{b}")
                    p.w(sync, f"QB{b}")
                    p.inc(gp.dma_start(out=wot[b][:],
                                       in_=woT[ch * 128:(ch + 1) * 128, g2 * 512:(g2 + 1) * 512]), f"QA{b}", 16)
                    p.inc(sync.dma_start(out=oft[b][:], in_=oscr[ch * 128:(ch + 1) * 128, :]), f"QB{b}", 16)
                    p.w(te, f"QA{b}")
                    p.w(te, f"QB{b}")
                    if ch == 0:
                        p.w(te, "SV")
                    for blk in range(4):
                        for hf in range(2):
                            ins = te.matmul(pw[blk * 2 + hf][:], wot[b][:, blk * 128:(blk + 1) * 128],
                                            oft[b][:, hf * 512:(hf + 1) * 512],
                                            start=(ch == 0), stop=(ch == 31))
                    guard[b] = p.inc(ins, "ST")
                for blk in range(4):
                    p.w(vec, "ST")
                    p.w(vec, "QD")
                    p.w(vec, "SV")
                    p.inc(vec.tensor_copy(wst[:, 0:512], pw[blk * 2][:]), "SV")
                    p.w(vec, "SV")
                    p.inc(vec.tensor_copy(wst[:, 512:1024], pw[blk * 2 + 1][:]), "SV")
                    p.w(sync, "SV")
                    p.w(sync, "QD")
                    r0 = g2 * 512 + blk * 128
                    p.inc(sync.dma_start(out=wooscr[r0:r0 + 128, :], in_=wst[:]), "QD", 16)
        barrier()

    def resid_ln(srcscr, resid_dram, out_token_dram, outT_dram, tag):
        with contextlib.ExitStack() as s4:
            sb4 = lambda n, sh, dt: s4.enter_context(nc.sbuf_tensor(n + tag, sh, dt))
            sl = sb4("sl", [128, 32 * 128], BF16)
            bufA = sb4("bufA", [128, D], F32)
            bufB = sb4("bufB", [128, D], F32)
            x1b = sb4("x1b", [128, D], BF16)
            mn = sb4("mn", [128, 1], F32)
            ssq = sb4("ssq", [128, 1], F32)
            sdv = sb4("sdv", [128, 1], F32)
            rstd = sb4("rstd", [128, 1], F32)
            epsb = sb4("epsb", [128, 1], F32)
            p.w(vec, "SV")
            p.inc(vec.memset(epsb[:], 1e-5), "SV")
            slT = sb4("slT", [128, D], BF16)
            with contextlib.ExitStack() as sp:
                ptr = [sp.enter_context(nc.psum_tensor(f"p4{tag}{j}", [128, 1024], BF16)) for j in range(2)]
                tg = {0: ("SV", 0), 1: ("SV", 0)}
                for i in range(NUNIT):
                    p.w(sync, "SV")
                    p.w(sync, "ST")
                    p.w(sync, "SS")
                    p.w(sync, "QD")
                    p.w(sync, "QC")
                    src = srcscr[:, i * 128:(i + 1) * 128].rearrange("(c q2) q -> q2 c q", q2=128)
                    slv = sl[:].rearrange("p (c q) -> p c q", q=128)
                    for cg in range(4):
                        p.inc(sync.dma_start(out=slv[:, cg * 8:(cg + 1) * 8, :],
                                             in_=src[:, cg * 8:(cg + 1) * 8, :]), "QC", 16)
                    p.inc(sync.dma_start(out=bufA[:], in_=resid_dram[i * 128:(i + 1) * 128, :]), "QC", 16)
                    qld = p.cnt["QC"]
                    for cg in range(4):
                        j = cg % 2
                        p.w(te, "QC", qld)
                        gs, gv = tg[j]
                        p.w(te, gs, gv)
                        for t in range(8):
                            ch = cg * 8 + t
                            ins = te.transpose(ptr[j][:, t * 128:(t + 1) * 128],
                                               sl[:, ch * 128:(ch + 1) * 128], identb[:])
                        tv = p.inc(ins, "ST")
                        p.w(sc, "ST", tv)
                        tg[j] = ("SS", p.inc(sc.copy(slT[:, cg * 1024:(cg + 1) * 1024], ptr[j][:]), "SS"))
                    p.w(vec, "SS")
                    p.w(vec, "QC", qld)
                    p.w(vec, "SV")
                    p.inc(vec.tensor_add(bufB[:], slT[:], bufA[:]), "SV")
                    p.w(vec, "SV")
                    p.inc(vec.reduce_sum(mn[:], bufB[:], AX.X), "SV")
                    p.w(vec, "SV")
                    p.inc(vec.tensor_scalar(mn[:], mn[:], 1.0 / D, None, ALU.mult), "SV")
                    p.w(vec, "SV")
                    p.inc(vec.tensor_scalar(bufA[:], bufB[:], mn[:, 0:1], None, ALU.subtract), "SV")
                    p.w(sc, "SV")
                    p.w(sc, "SS")
                    p.inc(sc.activation(bufB[:], bufA[:], AF.Square, accum_out=ssq[:, 0:1]), "SS")
                    p.w(sc, "SS")
                    p.inc(sc.activation(sdv[:], ssq[:], AF.Sqrt, bias=epsb[:, 0:1], scale=1.0 / D), "SS")
                    p.w(vec, "SS")
                    p.w(vec, "SV")
                    p.inc(vec.reciprocal(rstd[:], sdv[:]), "SV")
                    p.w(vec, "SV")
                    p.inc(vec.tensor_scalar(bufB[:], bufA[:], rstd[:, 0:1], None, ALU.mult), "SV")
                    svx1 = p.cnt["SV"]
                    p.w(sync, "SV", svx1)
                    p.w(sync, "QD")
                    p.inc(sync.dma_start(out=out_token_dram[i * 128:(i + 1) * 128, :], in_=bufB[:]), "QD", 16)
                    if outT_dram is not None:
                        p.w(vec, "SV")
                        p.inc(vec.tensor_copy(x1b[:], bufB[:]), "SV")
                        for cg in range(4):
                            j = cg % 2
                            p.w(te, "SV")
                            gs, gv = tg[j]
                            p.w(te, gs, gv)
                            for t in range(8):
                                ch = cg * 8 + t
                                ins = te.transpose(ptr[j][:, t * 128:(t + 1) * 128],
                                                   x1b[:, ch * 128:(ch + 1) * 128], identb[:])
                            tv = p.inc(ins, "ST")
                            p.w(sc, "ST", tv)
                            tg[j] = ("SS", p.inc(sc.copy(sl[:, cg * 1024:(cg + 1) * 1024], ptr[j][:]), "SS"))
                        p.w(sync, "SS")
                        p.w(sync, "QD")
                        dstT = outT_dram[:, i * 128:(i + 1) * 128].rearrange("(c q2) q -> q2 c q", q2=128)
                        for cg in range(4):
                            p.inc(sync.dma_start(out=dstT[:, cg * 8:(cg + 1) * 8, :],
                                                 in_=slv[:, cg * 8:(cg + 1) * 8, :]), "QD", 16)
            barrier()

    resid_ln(wooscr, xo, x1scr, x1Tscr, 'a')

    with contextlib.ExitStack() as s5:
        sb5 = lambda n, sh, dt: s5.enter_context(nc.sbuf_tensor(n, sh, dt))
        x1t = [sb5(f"x1t{j}", [128, OWN], BF16) for j in range(4)]
        wrt = [sb5(f"wrt{j}", [128, 64], BF16) for j in range(4)]
        Rsb = sb5("Rsb", [128, NUNIT * 64], F32)
        Rstg = sb5("Rstg", [64, OWN], BF16)
        t8a = sb5("t8a", [128, 8], F32)
        t8b = sb5("t8b", [128, 8], F32)
        rep = sb5("rep", [128, 64], F32)
        thr5 = sb5("thr5", [128, 1], F32)
        ngm5 = sb5("ngm5", [128, 1], F32)
        ee = sb5("ee", [128, 64], F32)
        msk = sb5("msk", [128, 64], F32)
        g0 = sb5("g0", [128, 64], F32)
        zg = sb5("zg", [128, 1], F32)
        rzg = sb5("rzg", [128, 1], F32)
        gnb = sb5("gnb", [128, 64], BF16)
        GT = sb5("GT", [64, NUNIT * 128], BF16)
        Gexp = sb5("Gexp", [128, 8 * 1024], BF16)
        hGT = sb5("hGT", [128, 8 * 1024], BF16)
        guard = {0: 0, 1: 0, 2: 0, 3: 0}
        with contextlib.ExitStack() as sp:
            pR = [sp.enter_context(nc.psum_tensor(f"pR{j}", [64, 512], F32)) for j in range(2)]
            for ch in range(32):
                b = ch % 4
                p.w(gp, "ST", guard[b])
                p.w(sync, "ST", guard[b])
                p.w(gp, f"QA{b}")
                p.w(sync, f"QB{b}")
                p.inc(gp.dma_start(out=wrt[b][:], in_=wrT[ch * 128:(ch + 1) * 128, :]), f"QA{b}", 16)
                p.inc(sync.dma_start(out=x1t[b][:], in_=x1Tscr[ch * 128:(ch + 1) * 128, :]), f"QB{b}", 16)
                p.w(te, f"QA{b}")
                p.w(te, f"QB{b}")
                for hf in range(2):
                    ins = te.matmul(pR[hf][:], wrt[b][:], x1t[b][:, hf * 512:(hf + 1) * 512],
                                    start=(ch == 0), stop=(ch == 31))
                guard[b] = p.inc(ins, "ST")
            p.w(vec, "ST")
            p.w(vec, "SV")
            p.inc(vec.tensor_copy(Rstg[:, 0:512], pR[0][:]), "SV")
            p.w(vec, "SV")
            p.inc(vec.tensor_copy(Rstg[:, 512:1024], pR[1][:]), "SV")
            svr = p.cnt["SV"]
            p.w(te, "SV", svr)
        with contextlib.ExitStack() as sp:
            pT = [sp.enter_context(nc.psum_tensor(f"pT{j}", [128, 64], BF16)) for j in range(2)]
            tg = {0: 0, 1: 0}
            for i in range(NUNIT):
                j = i % 2
                p.w(te, "SV", tg[j] if tg[j] else svr)
                ins = te.transpose(pT[j][:], Rstg[:, i * 128:(i + 1) * 128], identb[0:64, 0:64])
                tv = p.inc(ins, "ST")
                p.w(vec, "ST", tv)
                tg[j] = p.inc(vec.tensor_copy(Rsb[:, i * 64:(i + 1) * 64], pT[j][:]), "SV")
            p.w(te, "SV")
        with contextlib.ExitStack() as sp:
            pG = [sp.enter_context(nc.psum_tensor(f"pG{j}", [64, 128], BF16)) for j in range(2)]
            tg = {0: 0, 1: 0}
            for i in range(NUNIT):
                Ru = Rsb[:, i * 64:(i + 1) * 64]
                p.w(vec, "SV")
                p.inc(vec.max(out=t8a[:], in_=Ru), "SV")
                p.w(vec, "SV")
                p.inc(vec.reduce_max(ngm5[:], Ru, AX.X, negate=True), "SV")
                p.w(vec, "SV")
                p.inc(vec.match_replace(out=rep[:], in_to_replace=t8a[:], in_values=Ru,
                                        imm_value=NEG), "SV")
                p.w(vec, "SV")
                p.inc(vec.max(out=t8b[:], in_=rep[:]), "SV")
                p.w(vec, "SV")
                p.inc(vec.tensor_reduce(thr5[:], t8b[:], AX.X, ALU.min), "SV")
                svg = p.cnt["SV"]
                p.w(sc, "SV", svg)
                p.w(sc, "SS")
                p.inc(sc.activation(ee[:], Ru, AF.Exp, bias=ngm5[:, 0:1]), "SS")
                p.w(vec, "SS")
                p.w(vec, "SV")
                p.inc(vec.tensor_scalar(msk[:], Ru, thr5[:, 0:1], None, ALU.is_ge), "SV")
                p.w(vec, "SV")
                p.inc(vec.scalar_tensor_tensor(g0[:], ee[:], 1.0, msk[:], ALU.mult, ALU.mult,
                                               accum_out=zg[:, 0:1]), "SV")
                p.w(vec, "SV")
                p.inc(vec.reciprocal(rzg[:], zg[:]), "SV")
                p.w(vec, "SV")
                p.inc(vec.tensor_scalar(gnb[:], g0[:], rzg[:, 0:1], None, ALU.mult), "SV")
                svgn = p.cnt["SV"]
                j = i % 2
                p.w(te, "SV", svgn)
                if tg[j]:
                    p.w(te, "SV", tg[j])
                ins = te.transpose(pG[j][:], gnb[:], identb[:])
                tv = p.inc(ins, "ST")
                p.w(vec, "ST", tv)
                tg[j] = p.inc(vec.tensor_copy(GT[:, i * 128:(i + 1) * 128], pG[j][:]), "SV")
            p.w(te, "SV")
        with contextlib.ExitStack() as sp:
            pE = [sp.enter_context(nc.psum_tensor(f"pE{j}", [128, 512], F32)) for j in range(2)]
            tg = {0: 0, 1: 0}
            for n in range(16):
                krb, hf = n // 2, n % 2
                j = n % 2
                if tg[j]:
                    p.w(te, "SV", tg[j])
                ins = te.matmul(pE[j][:], exp16b[:, krb * 128:(krb + 1) * 128],
                                GT[:, hf * 512:(hf + 1) * 512], start=True, stop=True)
                tv = p.inc(ins, "ST")
                p.w(vec, "ST", tv)
                tg[j] = p.inc(vec.tensor_copy(Gexp[:, krb * 1024 + hf * 512: krb * 1024 + (hf + 1) * 512],
                                              pE[j][:]), "SV")
            p.w(te, "SV")
        barrier()

        with contextlib.ExitStack() as s6:
            at6 = [s6.enter_context(nc.sbuf_tensor(f"at6{j}", [128, 512], BF16)) for j in range(4)]
            guard = {0: 0, 1: 0, 2: 0, 3: 0}
            with contextlib.ExitStack() as sp:
                pH = [sp.enter_context(nc.psum_tensor(f"pH{j}", [128, 512], F32)) for j in range(8)]
                for g in range(2):
                    for ch in range(32):
                        b = ch % 4
                        p.w(gp, "ST", guard[b])
                        p.w(sync, "ST", guard[b])
                        p.w(gp, f"QA{b}")
                        p.w(sync, f"QB{b}")
                        p.inc(gp.dma_start(out=at6[b][:],
                                           in_=aT[ch * 128:(ch + 1) * 128, g * 512:(g + 1) * 512]), f"QA{b}", 16)
                        p.inc(sync.dma_start(out=x1t[b][:], in_=x1Tscr[ch * 128:(ch + 1) * 128, :]), f"QB{b}", 16)
                        p.w(te, f"QA{b}")
                        p.w(te, f"QB{b}")
                        if ch == 0:
                            p.w(te, "SV")
                        for k4 in range(4):
                            for hf in range(2):
                                ins = te.matmul(pH[k4 * 2 + hf][:], at6[b][:, k4 * 128:(k4 + 1) * 128],
                                                x1t[b][:, hf * 512:(hf + 1) * 512],
                                                start=(ch == 0), stop=(ch == 31))
                        guard[b] = p.inc(ins, "ST")
                    for k4 in range(4):
                        krb = g * 4 + k4
                        p.w(vec, "ST")
                        for hf in range(2):
                            p.w(vec, "SV")
                            p.inc(vec.scalar_tensor_tensor(
                                hGT[:, krb * 1024 + hf * 512: krb * 1024 + (hf + 1) * 512],
                                pH[k4 * 2 + hf][:], 2.0,
                                Gexp[:, krb * 1024 + hf * 512: krb * 1024 + (hf + 1) * 512],
                                ALU.mult, ALU.mult), "SV")
            barrier()

        with contextlib.ExitStack() as s7:
            bt7 = [s7.enter_context(nc.sbuf_tensor(f"bt7{j}", [128, 512], BF16)) for j in range(4)]
            fst = s7.enter_context(nc.sbuf_tensor("fst", [128, OWN], BF16))
            guard = {0: 0, 1: 0, 2: 0, 3: 0}
            with contextlib.ExitStack() as sp:
                pB = [sp.enter_context(nc.psum_tensor(f"pB{j}", [128, 512], F32)) for j in range(8)]
                for g2 in range(8):
                    for ch in range(8):
                        b = ch % 4
                        p.w(gp, "ST", guard[b])
                        p.w(gp, f"QA{b}")
                        p.inc(gp.dma_start(out=bt7[b][:],
                                           in_=bF[ch * 128:(ch + 1) * 128, g2 * 512:(g2 + 1) * 512]), f"QA{b}", 16)
                        p.w(te, f"QA{b}")
                        if ch == 0:
                            p.w(te, "SV")
                        for blk in range(4):
                            for hf in range(2):
                                ins = te.matmul(pB[blk * 2 + hf][:], bt7[b][:, blk * 128:(blk + 1) * 128],
                                                hGT[:, ch * 1024 + hf * 512: ch * 1024 + (hf + 1) * 512],
                                                start=(ch == 0), stop=(ch == 7))
                        guard[b] = p.inc(ins, "ST")
                    for blk in range(4):
                        p.w(vec, "ST")
                        p.w(vec, "QD")
                        p.w(vec, "SV")
                        p.inc(vec.tensor_copy(fst[:, 0:512], pB[blk * 2][:]), "SV")
                        p.w(vec, "SV")
                        p.inc(vec.tensor_copy(fst[:, 512:1024], pB[blk * 2 + 1][:]), "SV")
                        p.w(sync, "SV")
                        p.w(sync, "QD")
                        r0 = g2 * 512 + blk * 128
                        p.inc(sync.dma_start(out=ffnoscr[r0:r0 + 128, :], in_=fst[:]), "QD", 16)
            barrier()

    resid_ln(ffnoscr, x1scr, y, None, 'b')

    for eng in (sync, gp, vec, sc, te):
        p.w(eng, "QD")
        p.w(eng, "QC")


def host_inputs(inputs):
    import ml_dtypes
    xs = np.ascontiguousarray(inputs["L0_attn_hidden_states"])
    wq = inputs["L0_attn_wq"]; wk = inputs["L0_attn_wk"]; wv = inputs["L0_attn_wv"]
    wo = inputs["L0_attn_wo"]; A = inputs["L0_ffn_A"]; B = inputs["L0_ffn_B"]
    Wr = inputs["L0_ffn_Wr"]

    wqT = np.ascontiguousarray(wq.T)
    wkT = np.ascontiguousarray(wk.T)
    wvT = np.ascontiguousarray(wv.T)
    woT = np.ascontiguousarray(wo.T)
    aT = np.ascontiguousarray(A.reshape(1024, D).T)
    wrT = np.ascontiguousarray(Wr.T)
    bF = np.ascontiguousarray(B.transpose(0, 2, 1).reshape(1024, D))

    inv = (1.0 / (10000.0 ** (np.arange(0, HD, 2).astype(np.float32) / np.float32(HD)))).astype(np.float32)
    freqs = np.arange(S, dtype=np.float32)[:, None] * inv[None, :]
    emb = np.concatenate([freqs, freqs], axis=-1)
    cos_t = np.cos(emb).astype(np.float32)
    sin_t = np.sin(emb).astype(np.float32)
    sgn = np.ones(HD, np.float32); sgn[:64] = -1.0
    scale = np.float32(1.0 / np.sqrt(HD))

    rotP = np.zeros((HD, HD), np.float32)
    for d in range(HD):
        rotP[d, (d + 64) % HD] = 1.0
    rotPT = np.ascontiguousarray(rotP.T).astype(ml_dtypes.bfloat16)
    identv = np.eye(128, dtype=np.float32).astype(ml_dtypes.bfloat16)
    e16 = np.zeros((64, 1024), np.float32)
    for kr in range(1024):
        e16[kr // 16, kr] = 1.0
    e16 = e16.astype(ml_dtypes.bfloat16)
    e16g = np.zeros((128, 2048), np.float32)
    for k in range(2048):
        e16g[k // 16, k] = 1.0
    e16g = e16g.astype(ml_dtypes.bfloat16)

    cosk = np.ascontiguousarray(cos_t.T)
    sink = np.ascontiguousarray((sin_t * sgn[None, :]).T)

    in_maps, metas = [], []
    for c in range(8):
        b, par = c // 2, c % 2
        qbs = _units(par)
        rows = np.concatenate([np.arange(qb * 128, qb * 128 + 128) for qb in qbs])
        xb = xs[b]
        xov = np.ascontiguousarray(xb[rows])
        xoTv = np.ascontiguousarray(xov.T)
        xbTv = np.ascontiguousarray(xb.T)
        cosqv = np.ascontiguousarray((cos_t[rows] * scale).T)
        sinqv = np.ascontiguousarray((sin_t[rows] * scale * sgn[None, :]).T)
        cnegv = np.empty((NUNIT, 128, 384), np.float32)
        fixv = np.empty((NUNIT, 128, 384), np.float32)
        for i in range(NUNIT):
            qb = 2 * i + par
            Nk, W = _nk(i), _w(i)
            q_abs = qb * 128 + np.arange(128)[:, None]
            k_abs = Nk - W + np.arange(W)[None, :]
            causal = k_abs <= q_abs
            cnegv[i, :, :W] = np.where(causal, 0.0, NEG)
            fix01 = (causal & ((k_abs < 128) | (k_abs > q_abs - 128)))
            fixv[i, :, :W] = np.where(fix01, 0.0, NEG)
            if W < 384:
                cnegv[i, :, W:] = NEG
                fixv[i, :, W:] = NEG
        in_maps.append({
            "xo": xov, "xoT": xoTv, "xbT": xbTv,
            "wqT": wqT, "wkT": wkT, "wvT": wvT, "woT": woT,
            "aT": aT, "wrT": wrT, "bF": bF,
            "cosq": cosqv, "sinq": sinqv, "cosk": cosk, "sink": sink,
            "cnegs": cnegv.astype(ml_dtypes.bfloat16),
            "fixs": fixv.astype(ml_dtypes.bfloat16),
            "ident": identv, "rotPT": rotPT, "exp16": e16, "e16g": e16g,
        })
        metas.append((b, rows))
    return in_maps, metas


_CACHE = {}


def get_nc():
    if "nc" not in _CACHE:
        prog = P()
        with prog.es:
            with prog.nc.Block():
                build(prog)
        _CACHE["nc"] = prog.nc
    return _CACHE["nc"]


def kernel(**inputs):
    nc = get_nc()
    in_maps, metas = host_inputs(inputs)
    res = run_bass_kernel_spmd(nc, in_maps, core_ids=list(range(8)))
    out = np.zeros((4, S, D), np.float32)
    for c in range(8):
        b, rows = metas[c]
        out[b, rows] = np.asarray(res.results[c]["y"], dtype=np.float32)
    return out


# version 14
# speedup vs baseline: 1.0072x; 1.0072x over previous
import numpy as np
import contextlib
import concourse.bass as bass
import concourse.mybir as mybir
from concourse.bass_utils import run_bass_kernel_spmd

F32 = mybir.dt.float32
BF16 = mybir.dt.bfloat16
AF = mybir.ActivationFunctionType
ALU = mybir.AluOpType
AX = mybir.AxisListType

NEG = -1e30
D = 4096
HD = 128
NH = 32
NKV = 8
S = 2048
NUNIT = 8
OWN = 1024


def _units(p):
    return [2 * i + p for i in range(NUNIT)]


def _nk(i):
    return 256 * (i + 1)


def _w(i):
    return min(384, _nk(i))


class P:
    def __init__(self):
        self.nc = bass.Bass("TRN2", target_bir_lowering=False)
        self.es = contextlib.ExitStack()
        self.cnt = {}
        self.sems = {}

    def sem(self, name):
        h = self.es.enter_context(self.nc.semaphore(name))
        self.cnt[name] = 0
        self.sems[name] = h
        return h

    def inc(self, instr, name, k=1):
        instr.then_inc(self.sems[name], k)
        self.cnt[name] += k
        return self.cnt[name]

    def w(self, eng, name, val=None):
        eng.wait_ge(self.sems[name], self.cnt[name] if val is None else val)


def build(p: P):
    nc = p.nc
    din = lambda n, sh, dt=F32: nc.dram_tensor(n, sh, dt, kind="ExternalInput").ap()
    xo = din("xo", [OWN, D])
    xoT = din("xoT", [D, OWN])
    xbT = din("xbT", [D, S])
    wqT = din("wqT", [D, D])
    wkT = din("wkT", [D, 1024])
    wvT = din("wvT", [D, 1024])
    woT = din("woT", [D, D])
    aT = din("aT", [D, 1024])
    wrT = din("wrT", [D, 64])
    bF = din("bF", [1024, D])
    cosq = din("cosq", [HD, OWN])
    sinq = din("sinq", [HD, OWN])
    cosk = din("cosk", [HD, S])
    sink = din("sink", [HD, S])
    cnegs = din("cnegs", [NUNIT, 128, 384], BF16)
    fixs = din("fixs", [NUNIT, 128, 384], BF16)
    ident = din("ident", [128, 128], BF16)
    rotPT = din("rotPT", [128, 128], BF16)
    exp16 = din("exp16", [64, 1024], BF16)
    e16g = din("e16g", [128, 2048], BF16)
    y = nc.dram_tensor("y", [OWN, D], F32, kind="ExternalOutput").ap()

    oscr = nc.dram_tensor("oscr", [D, OWN], BF16, kind="Internal").ap()
    wooscr = nc.dram_tensor("wooscr", [OWN, D], BF16, kind="Internal").ap()
    x1scr = nc.dram_tensor("x1scr", [OWN, D], F32, kind="Internal").ap()
    x1Tscr = nc.dram_tensor("x1Tscr", [D, OWN], BF16, kind="Internal").ap()
    ffnoscr = nc.dram_tensor("ffnoscr", [OWN, D], BF16, kind="Internal").ap()

    SEMS = ("QA0", "QA1", "QA2", "QA3", "QB0", "QB1", "QB2", "QB3", "QC", "QD", "SV", "SS", "ST", "SG")
    for s_ in SEMS:
        p.sem(s_)
    sync, gp, vec, sc, te = nc.sync, nc.gpsimd, nc.vector, nc.scalar, nc.tensor

    def barrier():
        p.w(vec, "SV")
        p.inc(vec.nop(), "SV")
        p.w(sc, "SS")
        p.inc(sc.nop(), "SS")
        p.w(te, "ST")
        p.inc(te.nop(), "ST")
        p.w(gp, "SG")
        p.inc(gp.nop(), "SG")
        for eng in (sync, gp, vec, sc, te):
            for s_ in SEMS:
                p.w(eng, s_)

    sb = lambda n, sh, dt: p.es.enter_context(nc.sbuf_tensor(n, sh, dt))
    Kfm = sb("Kfm", [128, NKV * S], BF16)
    Vtm = sb("Vtm", [128, 16 * 1024], BF16)
    Qfm = sb("Qfm", [128, NH * OWN], BF16)
    cnegB = sb("cnegB", [128, NUNIT * 384], BF16)
    fixB = sb("fixB", [128, NUNIT * 384], BF16)
    identb = sb("identb", [128, 128], BF16)
    rotb = sb("rotb", [128, 128], BF16)
    exp16b = sb("exp16b", [64, 1024], BF16)
    e16gb = sb("e16gb", [128, 2048], BF16)

    for i in range(NUNIT):
        p.inc(sync.dma_start(out=cnegB[:, i * 384:(i + 1) * 384], in_=cnegs[i]), "QC", 16)
        p.inc(sync.dma_start(out=fixB[:, i * 384:(i + 1) * 384], in_=fixs[i]), "QC", 16)
    p.inc(sync.dma_start(out=identb[:], in_=ident), "QC", 16)
    p.inc(sync.dma_start(out=rotb[:], in_=rotPT), "QC", 16)
    p.inc(sync.dma_start(out=exp16b[:], in_=exp16), "QC", 16)
    p.inc(sync.dma_start(out=e16gb[:], in_=e16g), "QC", 16)
    p.w(te, "QC"); p.w(vec, "QC"); p.w(sc, "QC")

    with contextlib.ExitStack() as s1:
        sb1 = lambda n, sh, dt: s1.enter_context(nc.sbuf_tensor(n, sh, dt))
        ck = sb1("ck", [HD, S], F32)
        sk = sb1("sk", [HD, S], F32)
        p.w(sync, "QC")
        p.inc(sync.dma_start(out=ck[:], in_=cosk), "QC", 16)
        p.inc(sync.dma_start(out=sk[:], in_=sink), "QC", 16)
        p.w(vec, "QC")
        xbt = [sb1(f"xbt{j}", [128, 512], BF16) for j in range(4)]
        wkvt = [sb1(f"wkvt{j}", [128, 1024], BF16) for j in range(4)]
        vstage = sb1("vstage", [128, 8 * 512], BF16)
        rst1 = sb1("rst1", [128, 512], F32)
        rst2 = sb1("rst2", [128, 512], F32)
        kbst = [sb1(f"kbst{j}", [128, 512], BF16) for j in range(2)]
        guard = {0: 0, 1: 0, 2: 0, 3: 0}
        tr_guard = {0: ("SS", 0), 1: ("SS", 0)}
        for pas in range(2):
            wT = wkT if pas == 0 else wvT
            for tt in range(4):
                with contextlib.ExitStack() as sp:
                    pk = [sp.enter_context(nc.psum_tensor(f"pk{pas}{tt}{j}", [128, 512], F32))
                          for j in range(8)]
                    for ch in range(32):
                        b = ch % 4
                        p.w(gp, "ST", guard[b])
                        p.w(gp, f"QA{b}")
                        p.inc(gp.dma_start(out=xbt[b][:],
                                           in_=xbT[ch * 128:(ch + 1) * 128, tt * 512:(tt + 1) * 512]), f"QA{b}", 16)
                        p.inc(gp.dma_start(out=wkvt[b][:], in_=wT[ch * 128:(ch + 1) * 128, :]), f"QA{b}", 16)
                        p.w(te, f"QA{b}")
                        if ch == 0:
                            p.w(te, "SV")
                            p.w(te, "SS")
                        for kb in range(8):
                            ins = te.matmul(pk[kb][:], wkvt[b][:, kb * 128:(kb + 1) * 128], xbt[b][:],
                                            start=(ch == 0), stop=(ch == 31))
                        guard[b] = p.inc(ins, "ST")
                    if pas == 0:
                        cs = ck[:, tt * 512:(tt + 1) * 512]
                        sn = sk[:, tt * 512:(tt + 1) * 512]
                        rot_done = {0: 0, 1: 0}
                        for kb in range(8):
                            b2 = kb % 2
                            p.w(vec, "ST", rot_done[b2])
                            p.w(vec, "ST")
                            p.w(vec, "SV")
                            cpv = p.inc(vec.tensor_copy(kbst[b2][:], pk[kb][:]), "SV")
                            p.w(te, "SV", cpv)
                            rot_done[b2] = p.inc(te.matmul(pk[kb][:], rotb[:], kbst[b2][:],
                                                           start=True, stop=True), "ST")
                            p.w(vec, "SV")
                            p.inc(vec.tensor_mul(rst1[:], kbst[b2][:], cs), "SV")
                            p.w(vec, "ST", rot_done[b2])
                            p.w(vec, "SV")
                            p.inc(vec.tensor_mul(rst2[:], pk[kb][:], sn), "SV")
                            p.w(vec, "SV")
                            dst = Kfm[:, kb * S + tt * 512: kb * S + (tt + 1) * 512]
                            p.inc(vec.tensor_add(dst, rst1[:], rst2[:]), "SV")
                    else:
                        for kb in range(8):
                            p.w(vec, "ST")
                            p.w(vec, "SV")
                            p.inc(vec.tensor_copy(vstage[:, kb * 512:(kb + 1) * 512], pk[kb][:]), "SV")
                if pas == 1:
                    with contextlib.ExitStack() as sp:
                        trp = [sp.enter_context(nc.psum_tensor(f"trv{tt}{j}", [128, 1024], BF16))
                               for j in range(2)]
                        for tb in range(4):
                            j = tb % 2
                            gsem, gval = tr_guard[j]
                            p.w(te, gsem, gval)
                            p.w(te, "SV")
                            for vb in range(8):
                                ins = te.transpose(trp[j][:, vb * 128:(vb + 1) * 128],
                                                   vstage[:, vb * 512 + tb * 128: vb * 512 + (tb + 1) * 128],
                                                   identb[:])
                            tv = p.inc(ins, "ST")
                            p.w(sc, "ST", tv)
                            dst = Vtm[:, (tt * 4 + tb) * 1024: (tt * 4 + tb + 1) * 1024]
                            tr_guard[j] = ("SS", p.inc(sc.copy(dst, trp[j][:]), "SS"))
                        p.w(te, "SS")
        barrier()

    with contextlib.ExitStack() as s1:
        sb1 = lambda n, sh, dt: s1.enter_context(nc.sbuf_tensor(n, sh, dt))
        cq = sb1("cq", [HD, OWN], F32)
        sq_ = sb1("sq_", [HD, OWN], F32)
        p.w(sync, "QC")
        p.inc(sync.dma_start(out=cq[:], in_=cosq), "QC", 16)
        p.inc(sync.dma_start(out=sq_[:], in_=sinq), "QC", 16)
        p.w(vec, "QC")
        xot = [sb1(f"xot{j}", [128, OWN], BF16) for j in range(4)]
        wqt = [sb1(f"wqt{j}", [128, 512], BF16) for j in range(4)]
        qbst = [sb1(f"qbst{j}", [128, OWN], BF16) for j in range(2)]
        rst1 = sb1("rst1b", [128, 512], F32)
        rst2 = sb1("rst2b", [128, 512], F32)
        guard = {0: 0, 1: 0, 2: 0, 3: 0}
        with contextlib.ExitStack() as sp:
            pq = [sp.enter_context(nc.psum_tensor(f"pq{j}", [128, 512], F32)) for j in range(8)]
            for g in range(8):
                for ch in range(32):
                    b = ch % 4
                    p.w(gp, "ST", guard[b])
                    p.w(gp, f"QA{b}")
                    p.inc(gp.dma_start(out=xot[b][:], in_=xoT[ch * 128:(ch + 1) * 128, :]), f"QA{b}", 16)
                    p.inc(gp.dma_start(out=wqt[b][:],
                                       in_=wqT[ch * 128:(ch + 1) * 128, g * 512:(g + 1) * 512]), f"QA{b}", 16)
                    p.w(te, f"QA{b}")
                    if ch == 0:
                        p.w(te, "SV")
                    for ob in range(4):
                        for hf in range(2):
                            ins = te.matmul(pq[ob * 2 + hf][:], wqt[b][:, ob * 128:(ob + 1) * 128],
                                            xot[b][:, hf * 512:(hf + 1) * 512],
                                            start=(ch == 0), stop=(ch == 31))
                    guard[b] = p.inc(ins, "ST")
                rot_done = {0: 0, 1: 0}
                for ob in range(4):
                    h = g * 4 + ob
                    b2 = ob % 2
                    p.w(vec, "ST", rot_done[b2])
                    p.w(vec, "ST")
                    p.w(vec, "SV")
                    p.inc(vec.tensor_copy(qbst[b2][:, 0:512], pq[ob * 2][:]), "SV")
                    p.w(vec, "SV")
                    cpv = p.inc(vec.tensor_copy(qbst[b2][:, 512:1024], pq[ob * 2 + 1][:]), "SV")
                    p.w(te, "SV", cpv)
                    te.matmul(pq[ob * 2][:], rotb[:], qbst[b2][:, 0:512], start=True, stop=True)
                    p.w(te, "ST")
                    rot_done[b2] = p.inc(te.matmul(pq[ob * 2 + 1][:], rotb[:], qbst[b2][:, 512:1024],
                                                   start=True, stop=True), "ST")
                    for hf in range(2):
                        cs = cq[:, hf * 512:(hf + 1) * 512]
                        sn = sq_[:, hf * 512:(hf + 1) * 512]
                        p.w(vec, "SV")
                        p.inc(vec.tensor_mul(rst1[:], qbst[b2][:, hf * 512:(hf + 1) * 512], cs), "SV")
                        p.w(vec, "ST", rot_done[b2])
                        p.w(vec, "SV")
                        p.inc(vec.tensor_mul(rst2[:], pq[ob * 2 + hf][:], sn), "SV")
                        p.w(vec, "SV")
                        dst = Qfm[:, h * OWN + hf * 512: h * OWN + (hf + 1) * 512]
                        p.inc(vec.tensor_add(dst, rst1[:], rst2[:]), "SV")
        barrier()

    with contextlib.ExitStack() as s2:
        sb2 = lambda n, sh, dt: s2.enter_context(nc.sbuf_tensor(n, sh, dt))
        Ebuf = sb2("Ebuf", [128, 2048], BF16)
        PT = sb2("PT", [128, 16 * 512], BF16)
        Cm = sb2("Cm", [128, 128], F32)
        t8 = sb2("t8", [128, 8], F32)
        thr = sb2("thr", [128, 1], F32)
        ngm = sb2("ngm", [128, 1], F32)
        M01 = sb2("M01", [128, 128], F32)
        mnegb = sb2("mnegb", [128, 128], BF16)
        mnT = sb2("mnT", [128, 128], BF16)
        tqb = sb2("tqb", [128, 384], BF16)
        zparts = sb2("zparts", [128, 8], F32)
        zz = sb2("zz", [128, 1], F32)
        rz = sb2("rz", [128, 1], F32)
        rzb = [sb2(f"rzb{j}", [128, 1], BF16) for j in range(2)]
        dgrz = [sb2(f"dgrz{j}", [128, 128], BF16) for j in range(2)]
        ou = sb2("ou", [128, NH * 128], BF16)
        p.w(vec, "SV")
        p.inc(vec.memset(mnegb[:], 0.0), "SV")
        with contextlib.ExitStack() as sp:
            psS = sp.enter_context(nc.psum_tensor("psS", [128, 2048], F32))
            ptr = [sp.enter_context(nc.psum_tensor(f"ptr{j}", [128, 512], F32)) for j in range(2)]
            ptrM = sp.enter_context(nc.psum_tensor("ptrM", [128, 128], BF16))
            pso = sp.enter_context(nc.psum_tensor("pso", [128, 512], F32))
            bank_free = [("SV", 0)] * 4
            trg = {0: ("SV", 0), 1: ("SV", 0)}
            dg_guard = {0: 0, 1: 0}
            tmt_prev = 0
            tmask_prev = 0
            mnT_g = ("SV", 0)
            pend = None
            pend_o = None
            ss_oevac = 0
            t_av_prev = 0
            t_pt_last = 0

            def emit_pt_av(st):
                nonlocal t_av_prev, t_pt_last
                (i, h, ntile, ss_exp, sg_dg, kvh, hp) = st
                hh = h % 4
                ngr = (ntile + 3) // 4
                for g in range(ngr):
                    j = g % 2
                    nt = min(4, ntile - g * 4)
                    p.w(te, "SG", sg_dg)
                    gs, gv = trg[j]
                    p.w(te, gs, gv)
                    for t in range(nt):
                        kt = g * 4 + t
                        ins = te.matmul(ptr[j][:, t * 128:(t + 1) * 128],
                                        Ebuf[:, kt * 128:(kt + 1) * 128], dgrz[hp][:],
                                        start=True, stop=True)
                    tv = p.inc(ins, "ST")
                    p.w(sc, "ST", tv)
                    if hh == 0 and g == 0:
                        p.w(sc, "ST", t_av_prev)
                    dst = PT[:].rearrange("q2 (kt s) -> q2 kt s", s=512)
                    ev = p.inc(sc.copy(dst[:, g * 4: g * 4 + nt, hh * 128:(hh + 1) * 128],
                                       ptr[j][:, 0:nt * 128].rearrange("q2 (t s) -> q2 t s", s=128)), "SS")
                    trg[j] = ("SS", ev)
                t_pt_last = p.cnt["ST"]
                dg_guard[hp] = t_pt_last
                if hh == 3:
                    ev_all = p.cnt["SS"]
                    for kt in range(ntile):
                        p.w(te, "SS", ev_all)
                        if kt == 0:
                            p.w(te, "SS", ss_oevac)
                        vsl = Vtm[:, kt * 1024 + kvh * 128: kt * 1024 + (kvh + 1) * 128]
                        ins = te.matmul(pso[:], vsl, PT[:, kt * 512:(kt + 1) * 512],
                                        start=(kt == 0), stop=(kt == ntile - 1))
                    t_av = p.inc(ins, "ST")
                    t_av_prev = t_av
                    return (h, t_av)
                return None

            def emit_oevac(st):
                nonlocal ss_oevac
                (h, t_av) = st
                p.w(sc, "ST", t_av)
                p.w(sc, "SS")
                ss_oevac = p.inc(sc.copy(ou[:, (h - 3) * 128:(h + 1) * 128], pso[:]), "SS")

            for i in range(NUNIT):
                Nk, W = _nk(i), _w(i)
                nch, nbc = Nk // 16, W // 16
                nmid = Nk - W
                ncmid = nmid // 16
                ntile = Nk // 128
                mid_banks = []
                off = 0
                while off < nmid:
                    n = min(512, nmid - off)
                    mid_banks.append((off // 512, off, n))
                    off += n
                base = 0
                for h in range(NH):
                    kvh = h // 4
                    base = (h % 2) * 1024 if Nk <= 1024 else 0
                    qsl = Qfm[:, h * OWN + i * 128: h * OWN + (i + 1) * 128]
                    off = 0
                    while off < Nk:
                        n = min(512, Nk - off)
                        bank = (base + off) // 512
                        gs, gv = bank_free[bank]
                        p.w(te, gs, gv)
                        ins = te.matmul(psS[:, base + off: base + off + n], qsl,
                                        Kfm[:, kvh * S + off: kvh * S + off + n],
                                        start=True, stop=True)
                        p.inc(ins, "ST")
                        off += n
                    off = nmid
                    while off < Nk:
                        n = min(512 - (base + off) % 512, Nk - off)
                        p.w(te, "ST")
                        ins = te.matmul(psS[:, base + off: base + off + n], identb[:],
                                        cnegB[:, i * 384 + (off - nmid): i * 384 + (off - nmid) + n],
                                        start=False, stop=True, skip_group_check=True)
                        off += n
                    ts = p.inc(ins, "ST")
                    if pend is not None:
                        (pi, ph, pntile, pss_exp, pkvh, php, pzcol) = pend
                        p.w(vec, "SS", pss_exp)
                        p.w(vec, "SV")
                        p.inc(vec.tensor_reduce(zz[:], zparts[:, 0:pzcol], AX.X, ALU.add), "SV")
                        p.w(vec, "SV")
                        p.inc(vec.reciprocal(rz[:], zz[:]), "SV")
                        p.w(vec, "ST", dg_guard[php])
                        p.w(vec, "SV")
                        sv_rzb = p.inc(vec.tensor_copy(rzb[php][:], rz[:]), "SV")
                        p.w(gp, "SV", sv_rzb)
                        p.w(gp, "SG")
                        sg_dg = p.inc(gp.affine_select(out=dgrz[php][:],
                                                       in_=rzb[php][:, 0:1].to_broadcast([128, 128]),
                                                       pattern=[[-1, 128]], compare_op=ALU.is_equal,
                                                       fill=0.0, base=0, channel_multiplier=1), "SG")
                        pend_o = emit_pt_av((pi, ph, pntile, pss_exp, sg_dg, pkvh, php))
                        pend = None
                        if pend_o is not None:
                            emit_oevac(pend_o)
                            pend_o = None
                    if nmid > 0:
                        p.w(vec, "ST", ts)
                        p.w(vec, "SV")
                        src = psS[:, base: base + nmid].rearrange("p (c k) -> p c k", k=16)
                        p.inc(vec.tensor_reduce(Cm[:, 0:ncmid], src, AX.X, ALU.max), "SV")
                    p.w(vec, "ST", ts)
                    p.w(vec, "SV")
                    p.inc(vec.tensor_reduce(Cm[:, ncmid:nch],
                                            psS[:, base + nmid: base + Nk].rearrange("p (c k) -> p c k", k=16),
                                            AX.X, ALU.max), "SV")
                    sv_cmb = p.cnt["SV"]
                    p.w(vec, "SV")
                    p.inc(vec.max(out=t8[:], in_=Cm[:, 0:nch]), "SV")
                    p.w(vec, "SV")
                    p.inc(vec.tensor_scalar(ngm[:], t8[:, 0:1], -1.0, None, ALU.mult), "SV")
                    p.w(vec, "SV")
                    p.inc(vec.tensor_reduce(thr[:], t8[:], AX.X, ALU.min), "SV")
                    p.w(vec, "SV")
                    p.inc(vec.tensor_scalar(M01[:, 0:nch], Cm[:, 0:nch], thr[:, 0:1], None, ALU.is_ge), "SV")
                    p.w(vec, "SV")
                    p.inc(vec.memset(M01[:, 0:8], 1.0), "SV")
                    p.w(vec, "ST", tmt_prev)
                    p.w(vec, "SV")
                    p.inc(vec.tensor_scalar(mnegb[:, 0:nch], M01[:, 0:nch], 1.0, 1e30,
                                            ALU.subtract, ALU.mult), "SV")
                    sv_mneg = p.cnt["SV"]
                    mnbb = mnegb[:, nch - nbc:nch].to_broadcast([128, nbc, 16])
                    fx3 = fixB[:, i * 384: i * 384 + W].rearrange("p (c k) -> p c k", k=16)
                    p.w(vec, "ST", tmask_prev)
                    p.w(vec, "SV")
                    sv_tqb = p.inc(vec.tensor_tensor(tqb[:, 0:W].rearrange("p (c k) -> p c k", k=16),
                                                     fx3, mnbb, ALU.max), "SV")
                    if nmid > 0:
                        p.w(te, "SV", sv_mneg)
                        gs, gv = mnT_g
                        p.w(te, gs, gv)
                        tmt_prev = p.inc(te.transpose(ptrM[:], mnegb[:], identb[:]), "ST")
                        p.w(sc, "ST", tmt_prev)
                        ssm = p.inc(sc.copy(mnT[:], ptrM[:]), "SS")
                        mnT_g = ("SS", ssm)
                        p.w(te, "SS", ssm)
                        for (bank, off, n) in mid_banks:
                            ins = te.matmul(psS[:, base + off: base + off + n], mnT[:],
                                            e16gb[:, off:off + n], start=False, stop=True,
                                            skip_group_check=True)
                        p.inc(ins, "ST")
                    p.w(te, "SV", sv_tqb)
                    p.w(te, "SV", sv_cmb)
                    off = nmid
                    while off < Nk:
                        n = min(512 - (base + off) % 512, Nk - off)
                        p.w(te, "ST")
                        ins = te.matmul(psS[:, base + off: base + off + n], identb[:],
                                        tqb[:, off - nmid: off - nmid + n],
                                        start=False, stop=True, skip_group_check=True)
                        off += n
                    t_mask = p.inc(ins, "ST")
                    tmask_prev = t_mask
                    zcol = 0
                    p.w(sc, "ST", t_mask)
                    p.w(sc, "SV", sv_mneg)
                    for (bank, off, n) in mid_banks:
                        p.w(sc, "SS")
                        ssv = p.inc(sc.activation(Ebuf[:, off:off + n], psS[:, base + off: base + off + n],
                                                  AF.Exp, bias=ngm[:, 0:1],
                                                  accum_out=zparts[:, zcol:zcol + 1]), "SS")
                        zcol += 1
                        bank_free[(base + off) // 512] = ("SS", ssv)
                    off = nmid
                    while off < Nk:
                        n = min(512 - (base + off) % 512, Nk - off)
                        p.w(sc, "SS")
                        ssv = p.inc(sc.activation(Ebuf[:, off:off + n], psS[:, base + off: base + off + n],
                                                  AF.Exp, bias=ngm[:, 0:1],
                                                  accum_out=zparts[:, zcol:zcol + 1]), "SS")
                        zcol += 1
                        bank_free[(base + off) // 512] = ("SS", ssv)
                        off += n
                    ss_exp = p.cnt["SS"]
                    pend = (i, h, ntile, ss_exp, kvh, h % 2, zcol)
                (pi, ph, pntile, pss_exp, pkvh, php, pzcol) = pend
                p.w(vec, "SS", pss_exp)
                p.w(vec, "SV")
                p.inc(vec.tensor_reduce(zz[:], zparts[:, 0:pzcol], AX.X, ALU.add), "SV")
                p.w(vec, "SV")
                p.inc(vec.reciprocal(rz[:], zz[:]), "SV")
                p.w(vec, "ST", dg_guard[php])
                p.w(vec, "SV")
                sv_rzb = p.inc(vec.tensor_copy(rzb[php][:], rz[:]), "SV")
                p.w(gp, "SV", sv_rzb)
                p.w(gp, "SG")
                sg_dg = p.inc(gp.affine_select(out=dgrz[php][:],
                                               in_=rzb[php][:, 0:1].to_broadcast([128, 128]),
                                               pattern=[[-1, 128]], compare_op=ALU.is_equal,
                                               fill=0.0, base=0, channel_multiplier=1), "SG")
                emit_oevac(emit_pt_av((pi, ph, pntile, pss_exp, sg_dg, pkvh, php)))
                pend = None
                p.w(sync, "SS")
                p.w(sync, "QD")
                od = oscr.rearrange("(h d) q -> d h q", d=128)
                osrc = ou[:].rearrange("p (h q) -> p h q", q=128)
                for hg in range(4):
                    p.inc(sync.dma_start(out=od[:, hg * 8:(hg + 1) * 8, i * 128:(i + 1) * 128],
                                         in_=osrc[:, hg * 8:(hg + 1) * 8, :]), "QD", 16)
                p.w(sc, "QD")
        barrier()

    with contextlib.ExitStack() as s3:
        sb3 = lambda n, sh, dt: s3.enter_context(nc.sbuf_tensor(n, sh, dt))
        wot = [sb3(f"wot{j}", [128, 512], BF16) for j in range(4)]
        oft = [sb3(f"oft{j}", [128, OWN], BF16) for j in range(4)]
        wst = sb3("wst", [128, OWN], BF16)
        guard = {0: 0, 1: 0, 2: 0, 3: 0}
        with contextlib.ExitStack() as sp:
            pw = [sp.enter_context(nc.psum_tensor(f"pw{j}", [128, 512], F32)) for j in range(8)]
            for g2 in range(8):
                for ch in range(32):
                    b = ch % 4
                    p.w(gp, "ST", guard[b])
                    p.w(sync, "ST", guard[b])
                    p.w(gp, f"QA{b}")
                    p.w(sync, f"QB{b}")
                    p.inc(gp.dma_start(out=wot[b][:],
                                       in_=woT[ch * 128:(ch + 1) * 128, g2 * 512:(g2 + 1) * 512]), f"QA{b}", 16)
                    p.inc(sync.dma_start(out=oft[b][:], in_=oscr[ch * 128:(ch + 1) * 128, :]), f"QB{b}", 16)
                    p.w(te, f"QA{b}")
                    p.w(te, f"QB{b}")
                    if ch == 0:
                        p.w(te, "SV")
                    for blk in range(4):
                        for hf in range(2):
                            ins = te.matmul(pw[blk * 2 + hf][:], wot[b][:, blk * 128:(blk + 1) * 128],
                                            oft[b][:, hf * 512:(hf + 1) * 512],
                                            start=(ch == 0), stop=(ch == 31))
                    guard[b] = p.inc(ins, "ST")
                for blk in range(4):
                    p.w(vec, "ST")
                    p.w(vec, "QD")
                    p.w(vec, "SV")
                    p.inc(vec.tensor_copy(wst[:, 0:512], pw[blk * 2][:]), "SV")
                    p.w(vec, "SV")
                    p.inc(vec.tensor_copy(wst[:, 512:1024], pw[blk * 2 + 1][:]), "SV")
                    p.w(sync, "SV")
                    p.w(sync, "QD")
                    ch = g2 * 4 + blk
                    dst = wooscr.rearrange("(u r) (c q) -> r u c q", r=128, q=128)[:, :, ch, :]
                    p.inc(sync.dma_start(out=dst,
                                         in_=wst[:].rearrange("r (u q) -> r u q", q=128)), "QD", 16)
        barrier()

    def resid_ln(srcscr, resid_dram, out_token_dram, outT_dram, tag):
        with contextlib.ExitStack() as s4:
            sb4 = lambda n, sh, dt: s4.enter_context(nc.sbuf_tensor(n + tag, sh, dt))
            sl = sb4("sl", [128, 32 * 128], BF16)
            bufA = sb4("bufA", [128, D], F32)
            bufB = sb4("bufB", [128, D], F32)
            x1b = sb4("x1b", [128, D], BF16)
            mn = sb4("mn", [128, 1], F32)
            ssq = sb4("ssq", [128, 1], F32)
            sdv = sb4("sdv", [128, 1], F32)
            rstd = sb4("rstd", [128, 1], F32)
            epsb = sb4("epsb", [128, 1], F32)
            p.w(vec, "SV")
            p.inc(vec.memset(epsb[:], 1e-5), "SV")
            slT = sb4("slT", [128, D], BF16)
            with contextlib.ExitStack() as sp:
                ptr = [sp.enter_context(nc.psum_tensor(f"p4{tag}{j}", [128, 1024], BF16)) for j in range(2)]
                tg = {0: ("SV", 0), 1: ("SV", 0)}
                for i in range(NUNIT):
                    p.w(sync, "SV")
                    p.w(sync, "ST")
                    p.w(sync, "SS")
                    p.w(sync, "QD")
                    p.w(sync, "QC")
                    slv = sl[:].rearrange("p (c q) -> p c q", q=128)
                    p.inc(sync.dma_start(out=sl[:], in_=srcscr[i * 128:(i + 1) * 128, :]), "QC", 16)
                    p.inc(sync.dma_start(out=bufA[:], in_=resid_dram[i * 128:(i + 1) * 128, :]), "QC", 16)
                    qld = p.cnt["QC"]
                    for cg in range(4):
                        j = cg % 2
                        p.w(te, "QC", qld)
                        gs, gv = tg[j]
                        p.w(te, gs, gv)
                        for t in range(8):
                            ch = cg * 8 + t
                            ins = te.transpose(ptr[j][:, t * 128:(t + 1) * 128],
                                               sl[:, ch * 128:(ch + 1) * 128], identb[:])
                        tv = p.inc(ins, "ST")
                        p.w(sc, "ST", tv)
                        tg[j] = ("SS", p.inc(sc.copy(slT[:, cg * 1024:(cg + 1) * 1024], ptr[j][:]), "SS"))
                    p.w(vec, "SS")
                    p.w(vec, "QC", qld)
                    p.w(vec, "SV")
                    p.inc(vec.tensor_add(bufB[:], slT[:], bufA[:]), "SV")
                    p.w(vec, "SV")
                    p.inc(vec.reduce_sum(mn[:], bufB[:], AX.X), "SV")
                    p.w(vec, "SV")
                    p.inc(vec.tensor_scalar(mn[:], mn[:], 1.0 / D, None, ALU.mult), "SV")
                    p.w(vec, "SV")
                    p.inc(vec.tensor_scalar(bufA[:], bufB[:], mn[:, 0:1], None, ALU.subtract), "SV")
                    p.w(sc, "SV")
                    p.w(sc, "SS")
                    p.inc(sc.activation(bufB[:], bufA[:], AF.Square, accum_out=ssq[:, 0:1]), "SS")
                    p.w(sc, "SS")
                    p.inc(sc.activation(sdv[:], ssq[:], AF.Sqrt, bias=epsb[:, 0:1], scale=1.0 / D), "SS")
                    p.w(vec, "SS")
                    p.w(vec, "SV")
                    p.inc(vec.reciprocal(rstd[:], sdv[:]), "SV")
                    p.w(vec, "SV")
                    p.inc(vec.tensor_scalar(bufB[:], bufA[:], rstd[:, 0:1], None, ALU.mult), "SV")
                    svx1 = p.cnt["SV"]
                    p.w(sync, "SV", svx1)
                    p.w(sync, "QD")
                    p.inc(sync.dma_start(out=out_token_dram[i * 128:(i + 1) * 128, :], in_=bufB[:]), "QD", 16)
                    if outT_dram is not None:
                        p.w(vec, "SV")
                        p.inc(vec.tensor_copy(x1b[:], bufB[:]), "SV")
                        for cg in range(4):
                            j = cg % 2
                            p.w(te, "SV")
                            gs, gv = tg[j]
                            p.w(te, gs, gv)
                            for t in range(8):
                                ch = cg * 8 + t
                                ins = te.transpose(ptr[j][:, t * 128:(t + 1) * 128],
                                                   x1b[:, ch * 128:(ch + 1) * 128], identb[:])
                            tv = p.inc(ins, "ST")
                            p.w(sc, "ST", tv)
                            tg[j] = ("SS", p.inc(sc.copy(sl[:, cg * 1024:(cg + 1) * 1024], ptr[j][:]), "SS"))
                        p.w(sync, "SS")
                        p.w(sync, "QD")
                        dstT = outT_dram[:, i * 128:(i + 1) * 128].rearrange("(c q2) q -> q2 c q", q2=128)
                        for cg in range(4):
                            p.inc(sync.dma_start(out=dstT[:, cg * 8:(cg + 1) * 8, :],
                                                 in_=slv[:, cg * 8:(cg + 1) * 8, :]), "QD", 16)
            barrier()

    resid_ln(wooscr, xo, x1scr, x1Tscr, 'a')

    with contextlib.ExitStack() as s5:
        sb5 = lambda n, sh, dt: s5.enter_context(nc.sbuf_tensor(n, sh, dt))
        x1t = [sb5(f"x1t{j}", [128, OWN], BF16) for j in range(4)]
        wrt = [sb5(f"wrt{j}", [128, 64], BF16) for j in range(4)]
        Rsb = sb5("Rsb", [128, NUNIT * 64], F32)
        Rstg = sb5("Rstg", [64, OWN], BF16)
        t8a = sb5("t8a", [128, 8], F32)
        t8b = sb5("t8b", [128, 8], F32)
        rep = sb5("rep", [128, 64], F32)
        thr5 = sb5("thr5", [128, 1], F32)
        ngm5 = sb5("ngm5", [128, 1], F32)
        ee = sb5("ee", [128, 64], F32)
        msk = sb5("msk", [128, 64], F32)
        g0 = sb5("g0", [128, 64], F32)
        zg = sb5("zg", [128, 1], F32)
        rzg = sb5("rzg", [128, 1], F32)
        gnb = sb5("gnb", [128, 64], BF16)
        GT = sb5("GT", [64, NUNIT * 128], BF16)
        Gexp = sb5("Gexp", [128, 8 * 1024], BF16)
        hGT = sb5("hGT", [128, 8 * 1024], BF16)
        guard = {0: 0, 1: 0, 2: 0, 3: 0}
        with contextlib.ExitStack() as sp:
            pR = [sp.enter_context(nc.psum_tensor(f"pR{j}", [64, 512], F32)) for j in range(2)]
            for ch in range(32):
                b = ch % 4
                p.w(gp, "ST", guard[b])
                p.w(sync, "ST", guard[b])
                p.w(gp, f"QA{b}")
                p.w(sync, f"QB{b}")
                p.inc(gp.dma_start(out=wrt[b][:], in_=wrT[ch * 128:(ch + 1) * 128, :]), f"QA{b}", 16)
                p.inc(sync.dma_start(out=x1t[b][:], in_=x1Tscr[ch * 128:(ch + 1) * 128, :]), f"QB{b}", 16)
                p.w(te, f"QA{b}")
                p.w(te, f"QB{b}")
                for hf in range(2):
                    ins = te.matmul(pR[hf][:], wrt[b][:], x1t[b][:, hf * 512:(hf + 1) * 512],
                                    start=(ch == 0), stop=(ch == 31))
                guard[b] = p.inc(ins, "ST")
            p.w(vec, "ST")
            p.w(vec, "SV")
            p.inc(vec.tensor_copy(Rstg[:, 0:512], pR[0][:]), "SV")
            p.w(vec, "SV")
            p.inc(vec.tensor_copy(Rstg[:, 512:1024], pR[1][:]), "SV")
            svr = p.cnt["SV"]
            p.w(te, "SV", svr)
        with contextlib.ExitStack() as sp:
            pT = [sp.enter_context(nc.psum_tensor(f"pT{j}", [128, 64], BF16)) for j in range(2)]
            tg = {0: 0, 1: 0}
            for i in range(NUNIT):
                j = i % 2
                p.w(te, "SV", tg[j] if tg[j] else svr)
                ins = te.transpose(pT[j][:], Rstg[:, i * 128:(i + 1) * 128], identb[0:64, 0:64])
                tv = p.inc(ins, "ST")
                p.w(vec, "ST", tv)
                tg[j] = p.inc(vec.tensor_copy(Rsb[:, i * 64:(i + 1) * 64], pT[j][:]), "SV")
            p.w(te, "SV")
        with contextlib.ExitStack() as sp:
            pG = [sp.enter_context(nc.psum_tensor(f"pG{j}", [64, 128], BF16)) for j in range(2)]
            tg = {0: 0, 1: 0}
            for i in range(NUNIT):
                Ru = Rsb[:, i * 64:(i + 1) * 64]
                p.w(vec, "SV")
                p.inc(vec.max(out=t8a[:], in_=Ru), "SV")
                p.w(vec, "SV")
                p.inc(vec.reduce_max(ngm5[:], Ru, AX.X, negate=True), "SV")
                p.w(vec, "SV")
                p.inc(vec.match_replace(out=rep[:], in_to_replace=t8a[:], in_values=Ru,
                                        imm_value=NEG), "SV")
                p.w(vec, "SV")
                p.inc(vec.max(out=t8b[:], in_=rep[:]), "SV")
                p.w(vec, "SV")
                p.inc(vec.tensor_reduce(thr5[:], t8b[:], AX.X, ALU.min), "SV")
                svg = p.cnt["SV"]
                p.w(sc, "SV", svg)
                p.w(sc, "SS")
                p.inc(sc.activation(ee[:], Ru, AF.Exp, bias=ngm5[:, 0:1]), "SS")
                p.w(vec, "SS")
                p.w(vec, "SV")
                p.inc(vec.tensor_scalar(msk[:], Ru, thr5[:, 0:1], None, ALU.is_ge), "SV")
                p.w(vec, "SV")
                p.inc(vec.scalar_tensor_tensor(g0[:], ee[:], 1.0, msk[:], ALU.mult, ALU.mult,
                                               accum_out=zg[:, 0:1]), "SV")
                p.w(vec, "SV")
                p.inc(vec.reciprocal(rzg[:], zg[:]), "SV")
                p.w(vec, "SV")
                p.inc(vec.tensor_scalar(gnb[:], g0[:], rzg[:, 0:1], None, ALU.mult), "SV")
                svgn = p.cnt["SV"]
                j = i % 2
                p.w(te, "SV", svgn)
                if tg[j]:
                    p.w(te, "SV", tg[j])
                ins = te.transpose(pG[j][:], gnb[:], identb[:])
                tv = p.inc(ins, "ST")
                p.w(vec, "ST", tv)
                tg[j] = p.inc(vec.tensor_copy(GT[:, i * 128:(i + 1) * 128], pG[j][:]), "SV")
            p.w(te, "SV")
        with contextlib.ExitStack() as sp:
            pE = [sp.enter_context(nc.psum_tensor(f"pE{j}", [128, 512], F32)) for j in range(2)]
            tg = {0: 0, 1: 0}
            for n in range(16):
                krb, hf = n // 2, n % 2
                j = n % 2
                if tg[j]:
                    p.w(te, "SV", tg[j])
                ins = te.matmul(pE[j][:], exp16b[:, krb * 128:(krb + 1) * 128],
                                GT[:, hf * 512:(hf + 1) * 512], start=True, stop=True)
                tv = p.inc(ins, "ST")
                p.w(vec, "ST", tv)
                tg[j] = p.inc(vec.tensor_copy(Gexp[:, krb * 1024 + hf * 512: krb * 1024 + (hf + 1) * 512],
                                              pE[j][:]), "SV")
            p.w(te, "SV")
        barrier()

        with contextlib.ExitStack() as s6:
            at6 = [s6.enter_context(nc.sbuf_tensor(f"at6{j}", [128, 512], BF16)) for j in range(4)]
            guard = {0: 0, 1: 0, 2: 0, 3: 0}
            with contextlib.ExitStack() as sp:
                pH = [sp.enter_context(nc.psum_tensor(f"pH{j}", [128, 512], F32)) for j in range(8)]
                for g in range(2):
                    for ch in range(32):
                        b = ch % 4
                        p.w(gp, "ST", guard[b])
                        p.w(sync, "ST", guard[b])
                        p.w(gp, f"QA{b}")
                        p.w(sync, f"QB{b}")
                        p.inc(gp.dma_start(out=at6[b][:],
                                           in_=aT[ch * 128:(ch + 1) * 128, g * 512:(g + 1) * 512]), f"QA{b}", 16)
                        p.inc(sync.dma_start(out=x1t[b][:], in_=x1Tscr[ch * 128:(ch + 1) * 128, :]), f"QB{b}", 16)
                        p.w(te, f"QA{b}")
                        p.w(te, f"QB{b}")
                        if ch == 0:
                            p.w(te, "SV")
                        for k4 in range(4):
                            for hf in range(2):
                                ins = te.matmul(pH[k4 * 2 + hf][:], at6[b][:, k4 * 128:(k4 + 1) * 128],
                                                x1t[b][:, hf * 512:(hf + 1) * 512],
                                                start=(ch == 0), stop=(ch == 31))
                        guard[b] = p.inc(ins, "ST")
                    for k4 in range(4):
                        krb = g * 4 + k4
                        p.w(vec, "ST")
                        for hf in range(2):
                            p.w(vec, "SV")
                            p.inc(vec.scalar_tensor_tensor(
                                hGT[:, krb * 1024 + hf * 512: krb * 1024 + (hf + 1) * 512],
                                pH[k4 * 2 + hf][:], 2.0,
                                Gexp[:, krb * 1024 + hf * 512: krb * 1024 + (hf + 1) * 512],
                                ALU.mult, ALU.mult), "SV")
            barrier()

        with contextlib.ExitStack() as s7:
            bt7 = [s7.enter_context(nc.sbuf_tensor(f"bt7{j}", [128, 512], BF16)) for j in range(4)]
            fst = s7.enter_context(nc.sbuf_tensor("fst", [128, OWN], BF16))
            guard = {0: 0, 1: 0, 2: 0, 3: 0}
            with contextlib.ExitStack() as sp:
                pB = [sp.enter_context(nc.psum_tensor(f"pB{j}", [128, 512], F32)) for j in range(8)]
                for g2 in range(8):
                    for ch in range(8):
                        b = ch % 4
                        p.w(gp, "ST", guard[b])
                        p.w(gp, f"QA{b}")
                        p.inc(gp.dma_start(out=bt7[b][:],
                                           in_=bF[ch * 128:(ch + 1) * 128, g2 * 512:(g2 + 1) * 512]), f"QA{b}", 16)
                        p.w(te, f"QA{b}")
                        if ch == 0:
                            p.w(te, "SV")
                        for blk in range(4):
                            for hf in range(2):
                                ins = te.matmul(pB[blk * 2 + hf][:], bt7[b][:, blk * 128:(blk + 1) * 128],
                                                hGT[:, ch * 1024 + hf * 512: ch * 1024 + (hf + 1) * 512],
                                                start=(ch == 0), stop=(ch == 7))
                        guard[b] = p.inc(ins, "ST")
                    for blk in range(4):
                        p.w(vec, "ST")
                        p.w(vec, "QD")
                        p.w(vec, "SV")
                        p.inc(vec.tensor_copy(fst[:, 0:512], pB[blk * 2][:]), "SV")
                        p.w(vec, "SV")
                        p.inc(vec.tensor_copy(fst[:, 512:1024], pB[blk * 2 + 1][:]), "SV")
                        p.w(sync, "SV")
                        p.w(sync, "QD")
                        ch = g2 * 4 + blk
                        dst = ffnoscr.rearrange("(u r) (c q) -> r u c q", r=128, q=128)[:, :, ch, :]
                        p.inc(sync.dma_start(out=dst,
                                             in_=fst[:].rearrange("r (u q) -> r u q", q=128)), "QD", 16)
            barrier()

    resid_ln(ffnoscr, x1scr, y, None, 'b')

    for eng in (sync, gp, vec, sc, te):
        p.w(eng, "QD")
        p.w(eng, "QC")


def host_inputs(inputs):
    import ml_dtypes
    xs = np.ascontiguousarray(inputs["L0_attn_hidden_states"])
    wq = inputs["L0_attn_wq"]; wk = inputs["L0_attn_wk"]; wv = inputs["L0_attn_wv"]
    wo = inputs["L0_attn_wo"]; A = inputs["L0_ffn_A"]; B = inputs["L0_ffn_B"]
    Wr = inputs["L0_ffn_Wr"]

    wqT = np.ascontiguousarray(wq.T)
    wkT = np.ascontiguousarray(wk.T)
    wvT = np.ascontiguousarray(wv.T)
    woT = np.ascontiguousarray(wo.T)
    aT = np.ascontiguousarray(A.reshape(1024, D).T)
    wrT = np.ascontiguousarray(Wr.T)
    bF = np.ascontiguousarray(B.transpose(0, 2, 1).reshape(1024, D))

    inv = (1.0 / (10000.0 ** (np.arange(0, HD, 2).astype(np.float32) / np.float32(HD)))).astype(np.float32)
    freqs = np.arange(S, dtype=np.float32)[:, None] * inv[None, :]
    emb = np.concatenate([freqs, freqs], axis=-1)
    cos_t = np.cos(emb).astype(np.float32)
    sin_t = np.sin(emb).astype(np.float32)
    sgn = np.ones(HD, np.float32); sgn[:64] = -1.0
    scale = np.float32(1.0 / np.sqrt(HD))

    rotP = np.zeros((HD, HD), np.float32)
    for d in range(HD):
        rotP[d, (d + 64) % HD] = 1.0
    rotPT = np.ascontiguousarray(rotP.T).astype(ml_dtypes.bfloat16)
    identv = np.eye(128, dtype=np.float32).astype(ml_dtypes.bfloat16)
    e16 = np.zeros((64, 1024), np.float32)
    for kr in range(1024):
        e16[kr // 16, kr] = 1.0
    e16 = e16.astype(ml_dtypes.bfloat16)
    e16g = np.zeros((128, 2048), np.float32)
    for k in range(2048):
        e16g[k // 16, k] = 1.0
    e16g = e16g.astype(ml_dtypes.bfloat16)

    cosk = np.ascontiguousarray(cos_t.T)
    sink = np.ascontiguousarray((sin_t * sgn[None, :]).T)

    in_maps, metas = [], []
    for c in range(8):
        b, par = c // 2, c % 2
        qbs = _units(par)
        rows = np.concatenate([np.arange(qb * 128, qb * 128 + 128) for qb in qbs])
        xb = xs[b]
        xov = np.ascontiguousarray(xb[rows])
        xoTv = np.ascontiguousarray(xov.T)
        xbTv = np.ascontiguousarray(xb.T)
        cosqv = np.ascontiguousarray((cos_t[rows] * scale).T)
        sinqv = np.ascontiguousarray((sin_t[rows] * scale * sgn[None, :]).T)
        cnegv = np.empty((NUNIT, 128, 384), np.float32)
        fixv = np.empty((NUNIT, 128, 384), np.float32)
        for i in range(NUNIT):
            qb = 2 * i + par
            Nk, W = _nk(i), _w(i)
            q_abs = qb * 128 + np.arange(128)[:, None]
            k_abs = Nk - W + np.arange(W)[None, :]
            causal = k_abs <= q_abs
            cnegv[i, :, :W] = np.where(causal, 0.0, NEG)
            fix01 = (causal & ((k_abs < 128) | (k_abs > q_abs - 128)))
            fixv[i, :, :W] = np.where(fix01, 0.0, NEG)
            if W < 384:
                cnegv[i, :, W:] = NEG
                fixv[i, :, W:] = NEG
        in_maps.append({
            "xo": xov, "xoT": xoTv, "xbT": xbTv,
            "wqT": wqT, "wkT": wkT, "wvT": wvT, "woT": woT,
            "aT": aT, "wrT": wrT, "bF": bF,
            "cosq": cosqv, "sinq": sinqv, "cosk": cosk, "sink": sink,
            "cnegs": cnegv.astype(ml_dtypes.bfloat16),
            "fixs": fixv.astype(ml_dtypes.bfloat16),
            "ident": identv, "rotPT": rotPT, "exp16": e16, "e16g": e16g,
        })
        metas.append((b, rows))
    return in_maps, metas


_CACHE = {}


def get_nc():
    if "nc" not in _CACHE:
        prog = P()
        with prog.es:
            with prog.nc.Block():
                build(prog)
        _CACHE["nc"] = prog.nc
    return _CACHE["nc"]


def kernel(**inputs):
    nc = get_nc()
    in_maps, metas = host_inputs(inputs)
    res = run_bass_kernel_spmd(nc, in_maps, core_ids=list(range(8)))
    out = np.zeros((4, S, D), np.float32)
    for c in range(8):
        b, rows = metas[c]
        out[b, rows] = np.asarray(res.results[c]["y"], dtype=np.float32)
    return out


# version 15 (final)
# speedup vs baseline: 1.0632x; 1.0556x over previous
import numpy as np
import contextlib
import concourse.bass as bass
import concourse.mybir as mybir
from concourse.bass_utils import run_bass_kernel_spmd

F32 = mybir.dt.float32
BF16 = mybir.dt.bfloat16
AF = mybir.ActivationFunctionType
ALU = mybir.AluOpType
AX = mybir.AxisListType

NEG = -1e30
D = 4096
HD = 128
NH = 32
NKV = 8
S = 2048
NUNIT = 8
OWN = 1024


def _units(p):
    return [2 * i + p for i in range(NUNIT)]


def _nk(i):
    return 256 * (i + 1)


def _w(i):
    return min(384, _nk(i))


class P:
    def __init__(self):
        self.nc = bass.Bass("TRN2", target_bir_lowering=False)
        self.es = contextlib.ExitStack()
        self.cnt = {}
        self.sems = {}

    def sem(self, name):
        h = self.es.enter_context(self.nc.semaphore(name))
        self.cnt[name] = 0
        self.sems[name] = h
        return h

    def inc(self, instr, name, k=1):
        instr.then_inc(self.sems[name], k)
        self.cnt[name] += k
        return self.cnt[name]

    def w(self, eng, name, val=None):
        eng.wait_ge(self.sems[name], self.cnt[name] if val is None else val)


def build(p: P):
    nc = p.nc
    din = lambda n, sh, dt=F32: nc.dram_tensor(n, sh, dt, kind="ExternalInput").ap()
    xo = din("xo", [OWN, D])
    xoT = din("xoT", [D, OWN])
    xbT = din("xbT", [D, S])
    wqT = din("wqT", [D, D])
    wkT = din("wkT", [D, 1024])
    wvT = din("wvT", [D, 1024])
    woT = din("woT", [D, D])
    aT = din("aT", [D, 1024])
    wrT = din("wrT", [D, 64])
    bF = din("bF", [1024, D])
    cosq = din("cosq", [HD, OWN])
    sinq = din("sinq", [HD, OWN])
    cosk = din("cosk", [HD, S])
    sink = din("sink", [HD, S])
    cnegs = din("cnegs", [NUNIT, 128, 384], BF16)
    fixs = din("fixs", [NUNIT, 128, 384], BF16)
    ident = din("ident", [128, 128], BF16)
    rotPT = din("rotPT", [128, 128], BF16)
    exp16 = din("exp16", [64, 1024], BF16)
    e16g = din("e16g", [128, 2048], BF16)
    y = nc.dram_tensor("y", [OWN, D], F32, kind="ExternalOutput").ap()

    oscr = nc.dram_tensor("oscr", [D, OWN], BF16, kind="Internal").ap()
    wooscr = nc.dram_tensor("wooscr", [OWN, D], BF16, kind="Internal").ap()
    x1scr = nc.dram_tensor("x1scr", [OWN, D], F32, kind="Internal").ap()
    x1Tscr = nc.dram_tensor("x1Tscr", [D, OWN], BF16, kind="Internal").ap()
    ffnoscr = nc.dram_tensor("ffnoscr", [OWN, D], BF16, kind="Internal").ap()

    SEMS = ("QA0", "QA1", "QA2", "QA3", "QB0", "QB1", "QB2", "QB3", "QC", "QD", "SV", "SS", "ST", "SG")
    for s_ in SEMS:
        p.sem(s_)
    sync, gp, vec, sc, te = nc.sync, nc.gpsimd, nc.vector, nc.scalar, nc.tensor

    def barrier():
        p.w(vec, "SV")
        p.inc(vec.nop(), "SV")
        p.w(sc, "SS")
        p.inc(sc.nop(), "SS")
        p.w(te, "ST")
        p.inc(te.nop(), "ST")
        p.w(gp, "SG")
        p.inc(gp.nop(), "SG")
        for eng in (sync, gp, vec, sc, te):
            for s_ in SEMS:
                p.w(eng, s_)

    sb = lambda n, sh, dt: p.es.enter_context(nc.sbuf_tensor(n, sh, dt))
    Kfm = sb("Kfm", [128, NKV * S], BF16)
    Vtm = sb("Vtm", [128, 16 * 1024], BF16)
    Qfm = sb("Qfm", [128, NH * OWN], BF16)
    cnegB = sb("cnegB", [128, NUNIT * 384], BF16)
    fixB = sb("fixB", [128, NUNIT * 384], BF16)
    identb = sb("identb", [128, 128], BF16)
    rotb = sb("rotb", [128, 128], BF16)
    exp16b = sb("exp16b", [64, 1024], BF16)
    e16gb = sb("e16gb", [128, 2048], BF16)

    for i in range(NUNIT):
        p.inc(sync.dma_start(out=cnegB[:, i * 384:(i + 1) * 384], in_=cnegs[i]), "QC", 16)
        p.inc(sync.dma_start(out=fixB[:, i * 384:(i + 1) * 384], in_=fixs[i]), "QC", 16)
    p.inc(sync.dma_start(out=identb[:], in_=ident), "QC", 16)
    p.inc(sync.dma_start(out=rotb[:], in_=rotPT), "QC", 16)
    p.inc(sync.dma_start(out=exp16b[:], in_=exp16), "QC", 16)
    p.inc(sync.dma_start(out=e16gb[:], in_=e16g), "QC", 16)
    p.w(te, "QC"); p.w(vec, "QC"); p.w(sc, "QC")

    with contextlib.ExitStack() as s1:
        sb1 = lambda n, sh, dt: s1.enter_context(nc.sbuf_tensor(n, sh, dt))
        ck = sb1("ck", [HD, S], F32)
        sk = sb1("sk", [HD, S], F32)
        p.w(sync, "QC")
        p.inc(sync.dma_start(out=ck[:], in_=cosk), "QC", 16)
        p.inc(sync.dma_start(out=sk[:], in_=sink), "QC", 16)
        p.w(vec, "QC")
        xbt = [sb1(f"xbt{j}", [128, 512], BF16) for j in range(4)]
        wkvt = [sb1(f"wkvt{j}", [128, 1024], BF16) for j in range(4)]
        vstage = sb1("vstage", [128, 8 * 512], BF16)
        rst1 = sb1("rst1", [128, 512], F32)
        rst2 = sb1("rst2", [128, 512], F32)
        kbst = [sb1(f"kbst{j}", [128, 512], BF16) for j in range(2)]
        guard = {0: 0, 1: 0, 2: 0, 3: 0}
        tr_guard = {0: ("SS", 0), 1: ("SS", 0)}
        for pas in range(2):
            wT = wkT if pas == 0 else wvT
            for tt in range(4):
                with contextlib.ExitStack() as sp:
                    pk = [sp.enter_context(nc.psum_tensor(f"pk{pas}{tt}{j}", [128, 512], F32))
                          for j in range(8)]
                    for ch in range(32):
                        b = ch % 4
                        p.w(gp, "ST", guard[b])
                        p.w(gp, f"QA{b}")
                        p.inc(gp.dma_start(out=xbt[b][:],
                                           in_=xbT[ch * 128:(ch + 1) * 128, tt * 512:(tt + 1) * 512]), f"QA{b}", 16)
                        p.inc(gp.dma_start(out=wkvt[b][:], in_=wT[ch * 128:(ch + 1) * 128, :]), f"QA{b}", 16)
                        p.w(te, f"QA{b}")
                        if ch == 0:
                            p.w(te, "SV")
                            p.w(te, "SS")
                        for kb in range(8):
                            ins = te.matmul(pk[kb][:], wkvt[b][:, kb * 128:(kb + 1) * 128], xbt[b][:],
                                            start=(ch == 0), stop=(ch == 31))
                        guard[b] = p.inc(ins, "ST")
                    if pas == 0:
                        cs = ck[:, tt * 512:(tt + 1) * 512]
                        sn = sk[:, tt * 512:(tt + 1) * 512]
                        rot_done = {0: 0, 1: 0}
                        for kb in range(8):
                            b2 = kb % 2
                            p.w(vec, "ST", rot_done[b2])
                            p.w(vec, "ST")
                            p.w(vec, "SV")
                            cpv = p.inc(vec.tensor_copy(kbst[b2][:], pk[kb][:]), "SV")
                            p.w(te, "SV", cpv)
                            rot_done[b2] = p.inc(te.matmul(pk[kb][:], rotb[:], kbst[b2][:],
                                                           start=True, stop=True), "ST")
                            p.w(vec, "SV")
                            p.inc(vec.tensor_mul(rst1[:], kbst[b2][:], cs), "SV")
                            p.w(vec, "ST", rot_done[b2])
                            p.w(vec, "SV")
                            p.inc(vec.tensor_mul(rst2[:], pk[kb][:], sn), "SV")
                            p.w(vec, "SV")
                            dst = Kfm[:, kb * S + tt * 512: kb * S + (tt + 1) * 512]
                            p.inc(vec.tensor_add(dst, rst1[:], rst2[:]), "SV")
                    else:
                        for kb in range(8):
                            p.w(vec, "ST")
                            p.w(vec, "SV")
                            p.inc(vec.tensor_copy(vstage[:, kb * 512:(kb + 1) * 512], pk[kb][:]), "SV")
                if pas == 1:
                    with contextlib.ExitStack() as sp:
                        trp = [sp.enter_context(nc.psum_tensor(f"trv{tt}{j}", [128, 1024], BF16))
                               for j in range(2)]
                        for tb in range(4):
                            j = tb % 2
                            gsem, gval = tr_guard[j]
                            p.w(te, gsem, gval)
                            p.w(te, "SV")
                            for vb in range(8):
                                ins = te.transpose(trp[j][:, vb * 128:(vb + 1) * 128],
                                                   vstage[:, vb * 512 + tb * 128: vb * 512 + (tb + 1) * 128],
                                                   identb[:])
                            tv = p.inc(ins, "ST")
                            p.w(sc, "ST", tv)
                            dst = Vtm[:, (tt * 4 + tb) * 1024: (tt * 4 + tb + 1) * 1024]
                            tr_guard[j] = ("SS", p.inc(sc.copy(dst, trp[j][:]), "SS"))
                        p.w(te, "SS")
        barrier()

    with contextlib.ExitStack() as s1:
        sb1 = lambda n, sh, dt: s1.enter_context(nc.sbuf_tensor(n, sh, dt))
        cq = sb1("cq", [HD, OWN], F32)
        sq_ = sb1("sq_", [HD, OWN], F32)
        p.w(sync, "QC")
        p.inc(sync.dma_start(out=cq[:], in_=cosq), "QC", 16)
        p.inc(sync.dma_start(out=sq_[:], in_=sinq), "QC", 16)
        p.w(vec, "QC")
        xot = [sb1(f"xot{j}", [128, OWN], BF16) for j in range(4)]
        wqt = [sb1(f"wqt{j}", [128, 512], BF16) for j in range(4)]
        qbst = [sb1(f"qbst{j}", [128, OWN], BF16) for j in range(2)]
        rst1 = sb1("rst1b", [128, 512], F32)
        rst2 = sb1("rst2b", [128, 512], F32)
        guard = {0: 0, 1: 0, 2: 0, 3: 0}
        with contextlib.ExitStack() as sp:
            pq = [sp.enter_context(nc.psum_tensor(f"pq{j}", [128, 512], F32)) for j in range(8)]
            for g in range(8):
                for ch in range(32):
                    b = ch % 4
                    p.w(gp, "ST", guard[b])
                    p.w(gp, f"QA{b}")
                    p.inc(gp.dma_start(out=xot[b][:], in_=xoT[ch * 128:(ch + 1) * 128, :]), f"QA{b}", 16)
                    p.inc(gp.dma_start(out=wqt[b][:],
                                       in_=wqT[ch * 128:(ch + 1) * 128, g * 512:(g + 1) * 512]), f"QA{b}", 16)
                    p.w(te, f"QA{b}")
                    if ch == 0:
                        p.w(te, "SV")
                    for ob in range(4):
                        for hf in range(2):
                            ins = te.matmul(pq[ob * 2 + hf][:], wqt[b][:, ob * 128:(ob + 1) * 128],
                                            xot[b][:, hf * 512:(hf + 1) * 512],
                                            start=(ch == 0), stop=(ch == 31))
                    guard[b] = p.inc(ins, "ST")
                rot_done = {0: 0, 1: 0}
                for ob in range(4):
                    h = g * 4 + ob
                    b2 = ob % 2
                    p.w(vec, "ST", rot_done[b2])
                    p.w(vec, "ST")
                    p.w(vec, "SV")
                    p.inc(vec.tensor_copy(qbst[b2][:, 0:512], pq[ob * 2][:]), "SV")
                    p.w(vec, "SV")
                    cpv = p.inc(vec.tensor_copy(qbst[b2][:, 512:1024], pq[ob * 2 + 1][:]), "SV")
                    p.w(te, "SV", cpv)
                    te.matmul(pq[ob * 2][:], rotb[:], qbst[b2][:, 0:512], start=True, stop=True)
                    p.w(te, "ST")
                    rot_done[b2] = p.inc(te.matmul(pq[ob * 2 + 1][:], rotb[:], qbst[b2][:, 512:1024],
                                                   start=True, stop=True), "ST")
                    for hf in range(2):
                        cs = cq[:, hf * 512:(hf + 1) * 512]
                        sn = sq_[:, hf * 512:(hf + 1) * 512]
                        p.w(vec, "SV")
                        p.inc(vec.tensor_mul(rst1[:], qbst[b2][:, hf * 512:(hf + 1) * 512], cs), "SV")
                        p.w(vec, "ST", rot_done[b2])
                        p.w(vec, "SV")
                        p.inc(vec.tensor_mul(rst2[:], pq[ob * 2 + hf][:], sn), "SV")
                        p.w(vec, "SV")
                        dst = Qfm[:, h * OWN + hf * 512: h * OWN + (hf + 1) * 512]
                        p.inc(vec.tensor_add(dst, rst1[:], rst2[:]), "SV")
        barrier()

    with contextlib.ExitStack() as s2:
        sb2 = lambda n, sh, dt: s2.enter_context(nc.sbuf_tensor(n, sh, dt))
        Ebuf = sb2("Ebuf", [128, 2048], BF16)
        PT = sb2("PT", [128, 16 * 512], BF16)
        Cm = sb2("Cm", [128, 128], F32)
        t8 = sb2("t8", [128, 8], F32)
        thr = sb2("thr", [128, 1], F32)
        ngm = sb2("ngm", [128, 1], F32)
        M01 = sb2("M01", [128, 128], F32)
        mnegb = sb2("mnegb", [128, 128], BF16)
        mnT = sb2("mnT", [128, 128], BF16)
        tqb = sb2("tqb", [128, 384], BF16)
        zparts = sb2("zparts", [128, 8], F32)
        zz = sb2("zz", [128, 1], F32)
        rz = sb2("rz", [128, 1], F32)
        rzb = [sb2(f"rzb{j}", [128, 1], BF16) for j in range(2)]
        dgrz = [sb2(f"dgrz{j}", [128, 128], BF16) for j in range(2)]
        ou = sb2("ou", [128, NH * 128], BF16)
        p.w(vec, "SV")
        p.inc(vec.memset(mnegb[:], 0.0), "SV")
        with contextlib.ExitStack() as sp:
            psS = sp.enter_context(nc.psum_tensor("psS", [128, 2048], F32))
            ptr = [sp.enter_context(nc.psum_tensor(f"ptr{j}", [128, 512], F32)) for j in range(2)]
            ptrM = sp.enter_context(nc.psum_tensor("ptrM", [128, 128], BF16))
            pso = sp.enter_context(nc.psum_tensor("pso", [128, 512], F32))
            bank_free = [("SV", 0)] * 4
            trg = {0: ("SV", 0), 1: ("SV", 0)}
            dg_guard = {0: 0, 1: 0}
            tmt_prev = 0
            tmask_prev = 0
            mnT_g = ("SV", 0)
            pend = None
            pend_o = None
            ss_oevac = 0
            t_av_prev = 0
            t_pt_last = 0

            def emit_pt_av(st):
                nonlocal t_av_prev, t_pt_last
                (i, h, ntile, ss_exp, sg_dg, kvh, hp) = st
                hh = h % 4
                ngr = (ntile + 3) // 4
                for g in range(ngr):
                    j = g % 2
                    nt = min(4, ntile - g * 4)
                    p.w(te, "SG", sg_dg)
                    gs, gv = trg[j]
                    p.w(te, gs, gv)
                    for t in range(nt):
                        kt = g * 4 + t
                        ins = te.matmul(ptr[j][:, t * 128:(t + 1) * 128],
                                        Ebuf[:, kt * 128:(kt + 1) * 128], dgrz[hp][:],
                                        start=True, stop=True)
                    tv = p.inc(ins, "ST")
                    p.w(sc, "ST", tv)
                    if hh == 0 and g == 0:
                        p.w(sc, "ST", t_av_prev)
                    dst = PT[:].rearrange("q2 (kt s) -> q2 kt s", s=512)
                    ev = p.inc(sc.copy(dst[:, g * 4: g * 4 + nt, hh * 128:(hh + 1) * 128],
                                       ptr[j][:, 0:nt * 128].rearrange("q2 (t s) -> q2 t s", s=128)), "SS")
                    trg[j] = ("SS", ev)
                t_pt_last = p.cnt["ST"]
                dg_guard[hp] = t_pt_last
                if hh == 3:
                    ev_all = p.cnt["SS"]
                    for kt in range(ntile):
                        p.w(te, "SS", ev_all)
                        if kt == 0:
                            p.w(te, "SS", ss_oevac)
                        vsl = Vtm[:, kt * 1024 + kvh * 128: kt * 1024 + (kvh + 1) * 128]
                        ins = te.matmul(pso[:], vsl, PT[:, kt * 512:(kt + 1) * 512],
                                        start=(kt == 0), stop=(kt == ntile - 1))
                    t_av = p.inc(ins, "ST")
                    t_av_prev = t_av
                    return (h, t_av)
                return None

            def emit_oevac(st):
                nonlocal ss_oevac
                (h, t_av) = st
                p.w(sc, "ST", t_av)
                p.w(sc, "SS")
                ss_oevac = p.inc(sc.copy(ou[:, (h - 3) * 128:(h + 1) * 128], pso[:]), "SS")

            for i in range(NUNIT):
                Nk, W = _nk(i), _w(i)
                nch, nbc = Nk // 16, W // 16
                nmid = Nk - W
                ncmid = nmid // 16
                ntile = Nk // 128
                mid_banks = []
                off = 0
                while off < nmid:
                    n = min(512, nmid - off)
                    mid_banks.append((off // 512, off, n))
                    off += n
                base = 0
                for h in range(NH):
                    kvh = h // 4
                    base = (h % 2) * 1024 if Nk <= 1024 else 0
                    qsl = Qfm[:, h * OWN + i * 128: h * OWN + (i + 1) * 128]
                    off = 0
                    while off < Nk:
                        n = min(512, Nk - off)
                        bank = (base + off) // 512
                        gs, gv = bank_free[bank]
                        p.w(te, gs, gv)
                        ins = te.matmul(psS[:, base + off: base + off + n], qsl,
                                        Kfm[:, kvh * S + off: kvh * S + off + n],
                                        start=True, stop=True)
                        p.inc(ins, "ST")
                        off += n
                    off = nmid
                    while off < Nk:
                        n = min(512 - (base + off) % 512, Nk - off)
                        p.w(te, "ST")
                        ins = te.matmul(psS[:, base + off: base + off + n], identb[:],
                                        cnegB[:, i * 384 + (off - nmid): i * 384 + (off - nmid) + n],
                                        start=False, stop=True, skip_group_check=True)
                        off += n
                    ts = p.inc(ins, "ST")
                    if pend is not None:
                        (pi, ph, pntile, pss_exp, pkvh, php, pzcol) = pend
                        p.w(vec, "SS", pss_exp)
                        p.w(vec, "SV")
                        p.inc(vec.tensor_reduce(zz[:], zparts[:, 0:pzcol], AX.X, ALU.add), "SV")
                        p.w(vec, "SV")
                        p.inc(vec.reciprocal(rz[:], zz[:]), "SV")
                        p.w(vec, "ST", dg_guard[php])
                        p.w(vec, "SV")
                        sv_rzb = p.inc(vec.tensor_copy(rzb[php][:], rz[:]), "SV")
                        p.w(gp, "SV", sv_rzb)
                        p.w(gp, "SG")
                        sg_dg = p.inc(gp.affine_select(out=dgrz[php][:],
                                                       in_=rzb[php][:, 0:1].to_broadcast([128, 128]),
                                                       pattern=[[-1, 128]], compare_op=ALU.is_equal,
                                                       fill=0.0, base=0, channel_multiplier=1), "SG")
                        pend_o = emit_pt_av((pi, ph, pntile, pss_exp, sg_dg, pkvh, php))
                        pend = None
                        if pend_o is not None:
                            emit_oevac(pend_o)
                            pend_o = None
                    p.w(vec, "ST", ts)
                    p.w(vec, "SV")
                    p.inc(vec.tensor_reduce(Cm[:, 0:nch],
                                            psS[:, base: base + Nk].rearrange("p (c k) -> p c k", k=16),
                                            AX.X, ALU.max), "SV")
                    sv_cmb = p.cnt["SV"]
                    p.w(vec, "SV")
                    p.inc(vec.max(out=t8[:], in_=Cm[:, 0:nch]), "SV")
                    p.w(vec, "SV")
                    p.inc(vec.tensor_scalar(ngm[:], t8[:, 0:1], -1.0, None, ALU.mult), "SV")
                    p.w(vec, "SV")
                    p.inc(vec.tensor_reduce(thr[:], t8[:], AX.X, ALU.min), "SV")
                    p.w(vec, "ST", tmt_prev)
                    p.w(vec, "SV")
                    p.inc(vec.tensor_scalar(mnegb[:, 0:nch], Cm[:, 0:nch], thr[:, 0:1], -1e30,
                                            ALU.is_lt, ALU.mult), "SV")
                    p.w(vec, "SV")
                    p.inc(vec.memset(mnegb[:, 0:8], 0.0), "SV")
                    sv_mneg = p.cnt["SV"]
                    mnbb = mnegb[:, nch - nbc:nch].to_broadcast([128, nbc, 16])
                    fx3 = fixB[:, i * 384: i * 384 + W].rearrange("p (c k) -> p c k", k=16)
                    p.w(vec, "ST", tmask_prev)
                    p.w(vec, "SV")
                    sv_tqb = p.inc(vec.tensor_tensor(tqb[:, 0:W].rearrange("p (c k) -> p c k", k=16),
                                                     fx3, mnbb, ALU.max), "SV")
                    if nmid > 0:
                        p.w(te, "SV", sv_mneg)
                        gs, gv = mnT_g
                        p.w(te, gs, gv)
                        tmt_prev = p.inc(te.transpose(ptrM[:], mnegb[:], identb[:]), "ST")
                        p.w(sc, "ST", tmt_prev)
                        ssm = p.inc(sc.copy(mnT[:], ptrM[:]), "SS")
                        mnT_g = ("SS", ssm)
                        p.w(te, "SS", ssm)
                        for (bank, off, n) in mid_banks:
                            ins = te.matmul(psS[:, base + off: base + off + n], mnT[:],
                                            e16gb[:, off:off + n], start=False, stop=True,
                                            skip_group_check=True)
                        p.inc(ins, "ST")
                    p.w(te, "SV", sv_tqb)
                    p.w(te, "SV", sv_cmb)
                    off = nmid
                    while off < Nk:
                        n = min(512 - (base + off) % 512, Nk - off)
                        p.w(te, "ST")
                        ins = te.matmul(psS[:, base + off: base + off + n], identb[:],
                                        tqb[:, off - nmid: off - nmid + n],
                                        start=False, stop=True, skip_group_check=True)
                        off += n
                    t_mask = p.inc(ins, "ST")
                    tmask_prev = t_mask
                    zcol = 0
                    p.w(sc, "ST", t_mask)
                    p.w(sc, "SV", sv_mneg)
                    if nmid > 0:
                        p.w(sc, "SS")
                        ssv = p.inc(sc.activation(Ebuf[:, 0:nmid], psS[:, base: base + nmid],
                                                  AF.Exp, bias=ngm[:, 0:1],
                                                  accum_out=zparts[:, zcol:zcol + 1]), "SS")
                        zcol += 1
                        for bank in range(base // 512, (base + nmid + 511) // 512):
                            bank_free[bank] = ("SS", ssv)
                    p.w(sc, "SS")
                    ssv = p.inc(sc.activation(Ebuf[:, nmid:Nk], psS[:, base + nmid: base + Nk],
                                              AF.Exp, bias=ngm[:, 0:1],
                                              accum_out=zparts[:, zcol:zcol + 1]), "SS")
                    zcol += 1
                    for bank in range((base + nmid) // 512, (base + Nk + 511) // 512):
                        bank_free[bank] = ("SS", ssv)
                    ss_exp = p.cnt["SS"]
                    pend = (i, h, ntile, ss_exp, kvh, h % 2, zcol)
                (pi, ph, pntile, pss_exp, pkvh, php, pzcol) = pend
                p.w(vec, "SS", pss_exp)
                p.w(vec, "SV")
                p.inc(vec.tensor_reduce(zz[:], zparts[:, 0:pzcol], AX.X, ALU.add), "SV")
                p.w(vec, "SV")
                p.inc(vec.reciprocal(rz[:], zz[:]), "SV")
                p.w(vec, "ST", dg_guard[php])
                p.w(vec, "SV")
                sv_rzb = p.inc(vec.tensor_copy(rzb[php][:], rz[:]), "SV")
                p.w(gp, "SV", sv_rzb)
                p.w(gp, "SG")
                sg_dg = p.inc(gp.affine_select(out=dgrz[php][:],
                                               in_=rzb[php][:, 0:1].to_broadcast([128, 128]),
                                               pattern=[[-1, 128]], compare_op=ALU.is_equal,
                                               fill=0.0, base=0, channel_multiplier=1), "SG")
                emit_oevac(emit_pt_av((pi, ph, pntile, pss_exp, sg_dg, pkvh, php)))
                pend = None
                p.w(sync, "SS")
                p.w(sync, "QD")
                od = oscr.rearrange("(h d) q -> d h q", d=128)
                osrc = ou[:].rearrange("p (h q) -> p h q", q=128)
                for hg in range(4):
                    p.inc(sync.dma_start(out=od[:, hg * 8:(hg + 1) * 8, i * 128:(i + 1) * 128],
                                         in_=osrc[:, hg * 8:(hg + 1) * 8, :]), "QD", 16)
                p.w(sc, "QD")
        barrier()

    with contextlib.ExitStack() as s3:
        sb3 = lambda n, sh, dt: s3.enter_context(nc.sbuf_tensor(n, sh, dt))
        wot = [sb3(f"wot{j}", [128, 512], BF16) for j in range(4)]
        oft = [sb3(f"oft{j}", [128, OWN], BF16) for j in range(4)]
        wst = sb3("wst", [128, OWN], BF16)
        guard = {0: 0, 1: 0, 2: 0, 3: 0}
        with contextlib.ExitStack() as sp:
            pw = [sp.enter_context(nc.psum_tensor(f"pw{j}", [128, 512], F32)) for j in range(8)]
            for g2 in range(8):
                for ch in range(32):
                    b = ch % 4
                    p.w(gp, "ST", guard[b])
                    p.w(sync, "ST", guard[b])
                    p.w(gp, f"QA{b}")
                    p.w(sync, f"QB{b}")
                    p.inc(gp.dma_start(out=wot[b][:],
                                       in_=woT[ch * 128:(ch + 1) * 128, g2 * 512:(g2 + 1) * 512]), f"QA{b}", 16)
                    p.inc(sync.dma_start(out=oft[b][:], in_=oscr[ch * 128:(ch + 1) * 128, :]), f"QB{b}", 16)
                    p.w(te, f"QA{b}")
                    p.w(te, f"QB{b}")
                    if ch == 0:
                        p.w(te, "SV")
                    for blk in range(4):
                        for hf in range(2):
                            ins = te.matmul(pw[blk * 2 + hf][:], wot[b][:, blk * 128:(blk + 1) * 128],
                                            oft[b][:, hf * 512:(hf + 1) * 512],
                                            start=(ch == 0), stop=(ch == 31))
                    guard[b] = p.inc(ins, "ST")
                for blk in range(4):
                    p.w(vec, "ST")
                    p.w(vec, "QD")
                    p.w(vec, "SV")
                    p.inc(vec.tensor_copy(wst[:, 0:512], pw[blk * 2][:]), "SV")
                    p.w(vec, "SV")
                    p.inc(vec.tensor_copy(wst[:, 512:1024], pw[blk * 2 + 1][:]), "SV")
                    p.w(sync, "SV")
                    p.w(sync, "QD")
                    ch = g2 * 4 + blk
                    dst = wooscr.rearrange("(u r) (c q) -> r u c q", r=128, q=128)[:, :, ch, :]
                    p.inc(sync.dma_start(out=dst,
                                         in_=wst[:].rearrange("r (u q) -> r u q", q=128)), "QD", 16)
        barrier()

    def resid_ln(srcscr, resid_dram, out_token_dram, outT_dram, tag):
        with contextlib.ExitStack() as s4:
            sb4 = lambda n, sh, dt: s4.enter_context(nc.sbuf_tensor(n + tag, sh, dt))
            sl = sb4("sl", [128, 32 * 128], BF16)
            bufA = sb4("bufA", [128, D], F32)
            bufB = sb4("bufB", [128, D], F32)
            x1b = sb4("x1b", [128, D], BF16)
            mn = sb4("mn", [128, 1], F32)
            ssq = sb4("ssq", [128, 1], F32)
            sdv = sb4("sdv", [128, 1], F32)
            rstd = sb4("rstd", [128, 1], F32)
            epsb = sb4("epsb", [128, 1], F32)
            p.w(vec, "SV")
            p.inc(vec.memset(epsb[:], 1e-5), "SV")
            slT = sb4("slT", [128, D], BF16)
            with contextlib.ExitStack() as sp:
                ptr = [sp.enter_context(nc.psum_tensor(f"p4{tag}{j}", [128, 1024], BF16)) for j in range(2)]
                tg = {0: ("SV", 0), 1: ("SV", 0)}
                for i in range(NUNIT):
                    p.w(sync, "SV")
                    p.w(sync, "ST")
                    p.w(sync, "SS")
                    p.w(sync, "QD")
                    p.w(sync, "QC")
                    slv = sl[:].rearrange("p (c q) -> p c q", q=128)
                    p.inc(sync.dma_start(out=sl[:], in_=srcscr[i * 128:(i + 1) * 128, :]), "QC", 16)
                    p.inc(sync.dma_start(out=bufA[:], in_=resid_dram[i * 128:(i + 1) * 128, :]), "QC", 16)
                    qld = p.cnt["QC"]
                    for cg in range(4):
                        j = cg % 2
                        p.w(te, "QC", qld)
                        gs, gv = tg[j]
                        p.w(te, gs, gv)
                        for t in range(8):
                            ch = cg * 8 + t
                            ins = te.transpose(ptr[j][:, t * 128:(t + 1) * 128],
                                               sl[:, ch * 128:(ch + 1) * 128], identb[:])
                        tv = p.inc(ins, "ST")
                        p.w(sc, "ST", tv)
                        tg[j] = ("SS", p.inc(sc.copy(slT[:, cg * 1024:(cg + 1) * 1024], ptr[j][:]), "SS"))
                    p.w(vec, "SS")
                    p.w(vec, "QC", qld)
                    p.w(vec, "SV")
                    p.inc(vec.tensor_add(bufB[:], slT[:], bufA[:]), "SV")
                    p.w(vec, "SV")
                    p.inc(vec.reduce_sum(mn[:], bufB[:], AX.X), "SV")
                    p.w(vec, "SV")
                    p.inc(vec.tensor_scalar(mn[:], mn[:], 1.0 / D, None, ALU.mult), "SV")
                    p.w(vec, "SV")
                    p.inc(vec.tensor_scalar(bufA[:], bufB[:], mn[:, 0:1], None, ALU.subtract), "SV")
                    p.w(sc, "SV")
                    p.w(sc, "SS")
                    p.inc(sc.activation(bufB[:], bufA[:], AF.Square, accum_out=ssq[:, 0:1]), "SS")
                    p.w(sc, "SS")
                    p.inc(sc.activation(sdv[:], ssq[:], AF.Sqrt, bias=epsb[:, 0:1], scale=1.0 / D), "SS")
                    p.w(vec, "SS")
                    p.w(vec, "SV")
                    p.inc(vec.reciprocal(rstd[:], sdv[:]), "SV")
                    p.w(vec, "SV")
                    p.inc(vec.tensor_scalar(bufB[:], bufA[:], rstd[:, 0:1], None, ALU.mult), "SV")
                    svx1 = p.cnt["SV"]
                    p.w(sync, "SV", svx1)
                    p.w(sync, "QD")
                    p.inc(sync.dma_start(out=out_token_dram[i * 128:(i + 1) * 128, :], in_=bufB[:]), "QD", 16)
                    if outT_dram is not None:
                        p.w(vec, "SV")
                        p.inc(vec.tensor_copy(x1b[:], bufB[:]), "SV")
                        for cg in range(4):
                            j = cg % 2
                            p.w(te, "SV")
                            gs, gv = tg[j]
                            p.w(te, gs, gv)
                            for t in range(8):
                                ch = cg * 8 + t
                                ins = te.transpose(ptr[j][:, t * 128:(t + 1) * 128],
                                                   x1b[:, ch * 128:(ch + 1) * 128], identb[:])
                            tv = p.inc(ins, "ST")
                            p.w(sc, "ST", tv)
                            tg[j] = ("SS", p.inc(sc.copy(sl[:, cg * 1024:(cg + 1) * 1024], ptr[j][:]), "SS"))
                        p.w(sync, "SS")
                        p.w(sync, "QD")
                        dstT = outT_dram[:, i * 128:(i + 1) * 128].rearrange("(c q2) q -> q2 c q", q2=128)
                        for cg in range(4):
                            p.inc(sync.dma_start(out=dstT[:, cg * 8:(cg + 1) * 8, :],
                                                 in_=slv[:, cg * 8:(cg + 1) * 8, :]), "QD", 16)
            barrier()

    resid_ln(wooscr, xo, x1scr, x1Tscr, 'a')

    with contextlib.ExitStack() as s5:
        sb5 = lambda n, sh, dt: s5.enter_context(nc.sbuf_tensor(n, sh, dt))
        x1t = [sb5(f"x1t{j}", [128, OWN], BF16) for j in range(4)]
        wrt = [sb5(f"wrt{j}", [128, 64], BF16) for j in range(4)]
        Rsb = sb5("Rsb", [128, NUNIT * 64], F32)
        Rstg = sb5("Rstg", [64, OWN], BF16)
        t8a = sb5("t8a", [128, 8], F32)
        t8b = sb5("t8b", [128, 8], F32)
        rep = sb5("rep", [128, 64], F32)
        thr5 = sb5("thr5", [128, 1], F32)
        ngm5 = sb5("ngm5", [128, 1], F32)
        ee = sb5("ee", [128, 64], F32)
        msk = sb5("msk", [128, 64], F32)
        g0 = sb5("g0", [128, 64], F32)
        zg = sb5("zg", [128, 1], F32)
        rzg = sb5("rzg", [128, 1], F32)
        gnb = sb5("gnb", [128, 64], BF16)
        GT = sb5("GT", [64, NUNIT * 128], BF16)
        Gexp = sb5("Gexp", [128, 8 * 1024], BF16)
        hGT = sb5("hGT", [128, 8 * 1024], BF16)
        guard = {0: 0, 1: 0, 2: 0, 3: 0}
        with contextlib.ExitStack() as sp:
            pR = [sp.enter_context(nc.psum_tensor(f"pR{j}", [64, 512], F32)) for j in range(2)]
            for ch in range(32):
                b = ch % 4
                p.w(gp, "ST", guard[b])
                p.w(sync, "ST", guard[b])
                p.w(gp, f"QA{b}")
                p.w(sync, f"QB{b}")
                p.inc(gp.dma_start(out=wrt[b][:], in_=wrT[ch * 128:(ch + 1) * 128, :]), f"QA{b}", 16)
                p.inc(sync.dma_start(out=x1t[b][:], in_=x1Tscr[ch * 128:(ch + 1) * 128, :]), f"QB{b}", 16)
                p.w(te, f"QA{b}")
                p.w(te, f"QB{b}")
                for hf in range(2):
                    ins = te.matmul(pR[hf][:], wrt[b][:], x1t[b][:, hf * 512:(hf + 1) * 512],
                                    start=(ch == 0), stop=(ch == 31))
                guard[b] = p.inc(ins, "ST")
            p.w(vec, "ST")
            p.w(vec, "SV")
            p.inc(vec.tensor_copy(Rstg[:, 0:512], pR[0][:]), "SV")
            p.w(vec, "SV")
            p.inc(vec.tensor_copy(Rstg[:, 512:1024], pR[1][:]), "SV")
            svr = p.cnt["SV"]
            p.w(te, "SV", svr)
        with contextlib.ExitStack() as sp:
            pT = [sp.enter_context(nc.psum_tensor(f"pT{j}", [128, 64], BF16)) for j in range(2)]
            tg = {0: 0, 1: 0}
            for i in range(NUNIT):
                j = i % 2
                p.w(te, "SV", tg[j] if tg[j] else svr)
                ins = te.transpose(pT[j][:], Rstg[:, i * 128:(i + 1) * 128], identb[0:64, 0:64])
                tv = p.inc(ins, "ST")
                p.w(vec, "ST", tv)
                tg[j] = p.inc(vec.tensor_copy(Rsb[:, i * 64:(i + 1) * 64], pT[j][:]), "SV")
            p.w(te, "SV")
        with contextlib.ExitStack() as sp:
            pG = [sp.enter_context(nc.psum_tensor(f"pG{j}", [64, 128], BF16)) for j in range(2)]
            tg = {0: 0, 1: 0}
            for i in range(NUNIT):
                Ru = Rsb[:, i * 64:(i + 1) * 64]
                p.w(vec, "SV")
                p.inc(vec.max(out=t8a[:], in_=Ru), "SV")
                p.w(vec, "SV")
                p.inc(vec.reduce_max(ngm5[:], Ru, AX.X, negate=True), "SV")
                p.w(vec, "SV")
                p.inc(vec.match_replace(out=rep[:], in_to_replace=t8a[:], in_values=Ru,
                                        imm_value=NEG), "SV")
                p.w(vec, "SV")
                p.inc(vec.max(out=t8b[:], in_=rep[:]), "SV")
                p.w(vec, "SV")
                p.inc(vec.tensor_reduce(thr5[:], t8b[:], AX.X, ALU.min), "SV")
                svg = p.cnt["SV"]
                p.w(sc, "SV", svg)
                p.w(sc, "SS")
                p.inc(sc.activation(ee[:], Ru, AF.Exp, bias=ngm5[:, 0:1]), "SS")
                p.w(vec, "SS")
                p.w(vec, "SV")
                p.inc(vec.tensor_scalar(msk[:], Ru, thr5[:, 0:1], None, ALU.is_ge), "SV")
                p.w(vec, "SV")
                p.inc(vec.scalar_tensor_tensor(g0[:], ee[:], 1.0, msk[:], ALU.mult, ALU.mult,
                                               accum_out=zg[:, 0:1]), "SV")
                p.w(vec, "SV")
                p.inc(vec.reciprocal(rzg[:], zg[:]), "SV")
                p.w(vec, "SV")
                p.inc(vec.tensor_scalar(gnb[:], g0[:], rzg[:, 0:1], None, ALU.mult), "SV")
                svgn = p.cnt["SV"]
                j = i % 2
                p.w(te, "SV", svgn)
                if tg[j]:
                    p.w(te, "SV", tg[j])
                ins = te.transpose(pG[j][:], gnb[:], identb[:])
                tv = p.inc(ins, "ST")
                p.w(vec, "ST", tv)
                tg[j] = p.inc(vec.tensor_copy(GT[:, i * 128:(i + 1) * 128], pG[j][:]), "SV")
            p.w(te, "SV")
        with contextlib.ExitStack() as sp:
            pE = [sp.enter_context(nc.psum_tensor(f"pE{j}", [128, 512], F32)) for j in range(2)]
            tg = {0: 0, 1: 0}
            for n in range(16):
                krb, hf = n // 2, n % 2
                j = n % 2
                if tg[j]:
                    p.w(te, "SV", tg[j])
                ins = te.matmul(pE[j][:], exp16b[:, krb * 128:(krb + 1) * 128],
                                GT[:, hf * 512:(hf + 1) * 512], start=True, stop=True)
                tv = p.inc(ins, "ST")
                p.w(vec, "ST", tv)
                tg[j] = p.inc(vec.tensor_copy(Gexp[:, krb * 1024 + hf * 512: krb * 1024 + (hf + 1) * 512],
                                              pE[j][:]), "SV")
            p.w(te, "SV")
        barrier()

        with contextlib.ExitStack() as s6:
            at6 = [s6.enter_context(nc.sbuf_tensor(f"at6{j}", [128, 512], BF16)) for j in range(4)]
            guard = {0: 0, 1: 0, 2: 0, 3: 0}
            with contextlib.ExitStack() as sp:
                pH = [sp.enter_context(nc.psum_tensor(f"pH{j}", [128, 512], F32)) for j in range(8)]
                for g in range(2):
                    for ch in range(32):
                        b = ch % 4
                        p.w(gp, "ST", guard[b])
                        p.w(sync, "ST", guard[b])
                        p.w(gp, f"QA{b}")
                        p.w(sync, f"QB{b}")
                        p.inc(gp.dma_start(out=at6[b][:],
                                           in_=aT[ch * 128:(ch + 1) * 128, g * 512:(g + 1) * 512]), f"QA{b}", 16)
                        p.inc(sync.dma_start(out=x1t[b][:], in_=x1Tscr[ch * 128:(ch + 1) * 128, :]), f"QB{b}", 16)
                        p.w(te, f"QA{b}")
                        p.w(te, f"QB{b}")
                        if ch == 0:
                            p.w(te, "SV")
                        for k4 in range(4):
                            for hf in range(2):
                                ins = te.matmul(pH[k4 * 2 + hf][:], at6[b][:, k4 * 128:(k4 + 1) * 128],
                                                x1t[b][:, hf * 512:(hf + 1) * 512],
                                                start=(ch == 0), stop=(ch == 31))
                        guard[b] = p.inc(ins, "ST")
                    for k4 in range(4):
                        krb = g * 4 + k4
                        p.w(vec, "ST")
                        for hf in range(2):
                            p.w(vec, "SV")
                            p.inc(vec.scalar_tensor_tensor(
                                hGT[:, krb * 1024 + hf * 512: krb * 1024 + (hf + 1) * 512],
                                pH[k4 * 2 + hf][:], 2.0,
                                Gexp[:, krb * 1024 + hf * 512: krb * 1024 + (hf + 1) * 512],
                                ALU.mult, ALU.mult), "SV")
            barrier()

        with contextlib.ExitStack() as s7:
            bt7 = [s7.enter_context(nc.sbuf_tensor(f"bt7{j}", [128, 512], BF16)) for j in range(4)]
            fst = s7.enter_context(nc.sbuf_tensor("fst", [128, OWN], BF16))
            guard = {0: 0, 1: 0, 2: 0, 3: 0}
            with contextlib.ExitStack() as sp:
                pB = [sp.enter_context(nc.psum_tensor(f"pB{j}", [128, 512], F32)) for j in range(8)]
                for g2 in range(8):
                    for ch in range(8):
                        b = ch % 4
                        p.w(gp, "ST", guard[b])
                        p.w(gp, f"QA{b}")
                        p.inc(gp.dma_start(out=bt7[b][:],
                                           in_=bF[ch * 128:(ch + 1) * 128, g2 * 512:(g2 + 1) * 512]), f"QA{b}", 16)
                        p.w(te, f"QA{b}")
                        if ch == 0:
                            p.w(te, "SV")
                        for blk in range(4):
                            for hf in range(2):
                                ins = te.matmul(pB[blk * 2 + hf][:], bt7[b][:, blk * 128:(blk + 1) * 128],
                                                hGT[:, ch * 1024 + hf * 512: ch * 1024 + (hf + 1) * 512],
                                                start=(ch == 0), stop=(ch == 7))
                        guard[b] = p.inc(ins, "ST")
                    for blk in range(4):
                        p.w(vec, "ST")
                        p.w(vec, "QD")
                        p.w(vec, "SV")
                        p.inc(vec.tensor_copy(fst[:, 0:512], pB[blk * 2][:]), "SV")
                        p.w(vec, "SV")
                        p.inc(vec.tensor_copy(fst[:, 512:1024], pB[blk * 2 + 1][:]), "SV")
                        p.w(sync, "SV")
                        p.w(sync, "QD")
                        ch = g2 * 4 + blk
                        dst = ffnoscr.rearrange("(u r) (c q) -> r u c q", r=128, q=128)[:, :, ch, :]
                        p.inc(sync.dma_start(out=dst,
                                             in_=fst[:].rearrange("r (u q) -> r u q", q=128)), "QD", 16)
            barrier()

    resid_ln(ffnoscr, x1scr, y, None, 'b')

    for eng in (sync, gp, vec, sc, te):
        p.w(eng, "QD")
        p.w(eng, "QC")


def host_inputs(inputs):
    import ml_dtypes
    xs = np.ascontiguousarray(inputs["L0_attn_hidden_states"])
    wq = inputs["L0_attn_wq"]; wk = inputs["L0_attn_wk"]; wv = inputs["L0_attn_wv"]
    wo = inputs["L0_attn_wo"]; A = inputs["L0_ffn_A"]; B = inputs["L0_ffn_B"]
    Wr = inputs["L0_ffn_Wr"]

    wqT = np.ascontiguousarray(wq.T)
    wkT = np.ascontiguousarray(wk.T)
    wvT = np.ascontiguousarray(wv.T)
    woT = np.ascontiguousarray(wo.T)
    aT = np.ascontiguousarray(A.reshape(1024, D).T)
    wrT = np.ascontiguousarray(Wr.T)
    bF = np.ascontiguousarray(B.transpose(0, 2, 1).reshape(1024, D))

    inv = (1.0 / (10000.0 ** (np.arange(0, HD, 2).astype(np.float32) / np.float32(HD)))).astype(np.float32)
    freqs = np.arange(S, dtype=np.float32)[:, None] * inv[None, :]
    emb = np.concatenate([freqs, freqs], axis=-1)
    cos_t = np.cos(emb).astype(np.float32)
    sin_t = np.sin(emb).astype(np.float32)
    sgn = np.ones(HD, np.float32); sgn[:64] = -1.0
    scale = np.float32(1.0 / np.sqrt(HD))

    rotP = np.zeros((HD, HD), np.float32)
    for d in range(HD):
        rotP[d, (d + 64) % HD] = 1.0
    rotPT = np.ascontiguousarray(rotP.T).astype(ml_dtypes.bfloat16)
    identv = np.eye(128, dtype=np.float32).astype(ml_dtypes.bfloat16)
    e16 = np.zeros((64, 1024), np.float32)
    for kr in range(1024):
        e16[kr // 16, kr] = 1.0
    e16 = e16.astype(ml_dtypes.bfloat16)
    e16g = np.zeros((128, 2048), np.float32)
    for k in range(2048):
        e16g[k // 16, k] = 1.0
    e16g = e16g.astype(ml_dtypes.bfloat16)

    cosk = np.ascontiguousarray(cos_t.T)
    sink = np.ascontiguousarray((sin_t * sgn[None, :]).T)

    in_maps, metas = [], []
    for c in range(8):
        b, par = c // 2, c % 2
        qbs = _units(par)
        rows = np.concatenate([np.arange(qb * 128, qb * 128 + 128) for qb in qbs])
        xb = xs[b]
        xov = np.ascontiguousarray(xb[rows])
        xoTv = np.ascontiguousarray(xov.T)
        xbTv = np.ascontiguousarray(xb.T)
        cosqv = np.ascontiguousarray((cos_t[rows] * scale).T)
        sinqv = np.ascontiguousarray((sin_t[rows] * scale * sgn[None, :]).T)
        cnegv = np.empty((NUNIT, 128, 384), np.float32)
        fixv = np.empty((NUNIT, 128, 384), np.float32)
        for i in range(NUNIT):
            qb = 2 * i + par
            Nk, W = _nk(i), _w(i)
            q_abs = qb * 128 + np.arange(128)[:, None]
            k_abs = Nk - W + np.arange(W)[None, :]
            causal = k_abs <= q_abs
            cnegv[i, :, :W] = np.where(causal, 0.0, NEG)
            fix01 = (causal & ((k_abs < 128) | (k_abs > q_abs - 128)))
            fixv[i, :, :W] = np.where(fix01, 0.0, NEG)
            if W < 384:
                cnegv[i, :, W:] = NEG
                fixv[i, :, W:] = NEG
        in_maps.append({
            "xo": xov, "xoT": xoTv, "xbT": xbTv,
            "wqT": wqT, "wkT": wkT, "wvT": wvT, "woT": woT,
            "aT": aT, "wrT": wrT, "bF": bF,
            "cosq": cosqv, "sinq": sinqv, "cosk": cosk, "sink": sink,
            "cnegs": cnegv.astype(ml_dtypes.bfloat16),
            "fixs": fixv.astype(ml_dtypes.bfloat16),
            "ident": identv, "rotPT": rotPT, "exp16": e16, "e16g": e16g,
        })
        metas.append((b, rows))
    return in_maps, metas


_CACHE = {}


def get_nc():
    if "nc" not in _CACHE:
        prog = P()
        with prog.es:
            with prog.nc.Block():
                build(prog)
        _CACHE["nc"] = prog.nc
    return _CACHE["nc"]


def kernel(**inputs):
    nc = get_nc()
    in_maps, metas = host_inputs(inputs)
    res = run_bass_kernel_spmd(nc, in_maps, core_ids=list(range(8)))
    out = np.zeros((4, S, D), np.float32)
    for c in range(8):
        b, rows = metas[c]
        out[b, rows] = np.asarray(res.results[c]["y"], dtype=np.float32)
    return out
